# version 15
# baseline (speedup 1.0000x reference)
import contextlib
import os
import numpy as np

import concourse.bass as bass
import concourse.mybir as mybir
from concourse.bass_utils import run_bass_kernel_spmd

F32 = mybir.dt.float32
BF16 = mybir.dt.bfloat16
I32 = mybir.dt.int32
Alu = mybir.AluOpType
AF = mybir.ActivationFunctionType
AX = mybir.AxisListType

PARANOID = bool(int(os.environ.get("KPARANOID", "0")))
T = 2048
C = 1024
H = 16
D = 64
HID = 4096
NT = T // 128
NCC = C // 128
NJ = HID // 128
NATT = 3 * C
NEG = -1.0e38
EPS_LN = 1e-5


class SemT:

    def __init__(self, nc, es, name):
        self.h = es.enter_context(nc.semaphore(name))
        self.n = 0

    def inc(self, instr, k=1):
        instr.then_inc(self.h, k)
        self.n += k
        return self.n

    def wait(self, eng):
        eng.wait_ge(self.h, self.n)

    def wait_at(self, eng, val):
        if val > 0:
            eng.wait_ge(self.h, val)


class G:
    pass


def VC(g, eng, instr):
    g.sQ.inc(instr)
    g.sQ.wait(eng)
    return instr


def chained(g, eng):
    class _P:
        def __getattr__(self, name):
            f = getattr(eng, name)

            def call(*a, **k):
                return VC(g, eng, f(*a, **k))
            return call
    return _P()


class DSlots:

    def __init__(self, g, name, n):
        self.sems = [SemT(g.nc, g.es, "%s%d" % (name, i)) for i in range(n)]

    def inc(self, instr, slot):
        return (slot, self.sems[slot].inc(instr, 16))

    def wait(self, eng, mark):
        if mark == 0 or mark is None:
            return
        slot, val = mark
        self.sems[slot].wait_at(eng, val)

    def final_wait(self, eng):
        for s in self.sems:
            s.wait(eng)


def build_nc(debug=False, stop_after=None):
    nc = bass.Bass("TRN2", target_bir_lowering=False,
                   detect_race_conditions=False)
    g = G()
    g.nc = nc
    g.debug = debug
    g.stop_after = stop_after

    g.x_d = nc.dram_tensor("x", [T, C], F32, kind="ExternalInput").ap()
    g.wattnT_d = nc.dram_tensor("wattnT", [C, NATT], F32, kind="ExternalInput").ap()
    g.wprojT_d = nc.dram_tensor("wprojT", [C, C], F32, kind="ExternalInput").ap()
    g.w1cm_d = nc.dram_tensor("w1cm", [NJ, C, 128], F32, kind="ExternalInput").ap()
    g.w2T_d = nc.dram_tensor("w2T", [HID, C], F32, kind="ExternalInput").ap()
    g.convw_d = nc.dram_tensor("convw", [NJ, 128, 3], F32, kind="ExternalInput").ap()
    g.convb_d = nc.dram_tensor("convb", [NJ, 128], F32, kind="ExternalInput").ap()
    g.out_d = nc.dram_tensor("out", [T, C], F32, kind="ExternalOutput").ap()

    g.h3scr = nc.dram_tensor("h3scr", [NJ, 128, T], F32, kind="Internal").ap()
    g.wqascr = nc.dram_tensor("wqascr", [NCC, 128, NATT], BF16, kind="Internal").ap()
    g.w2qscr = nc.dram_tensor("w2qscr", [NJ, 128, C], BF16, kind="Internal").ap()

    g.dbg = {}
    if debug:
        for name, shape, dt_ in [
            ("dbg_xnT", [NCC * 128, T], BF16), ("dbg_aT", [NCC * 128, T], BF16),
            ("dbg_QT", [NCC * 128, T], BF16), ("dbg_KT", [NCC * 128, T], BF16),
            ("dbg_Vb", [NT * 128, H * 66], BF16), ("dbg_scales", [128, 16], F32),
            ("dbg_YT", [NCC * 128, T], BF16), ("dbg_attnT", [NCC * 128, T], F32),
            ("dbg_s2b", [128, T], F32), ("dbg_cb", [128, 64], F32),
        ]:
            g.dbg[name] = nc.dram_tensor(name, shape, dt_, kind="ExternalOutput").ap()

    with contextlib.ExitStack() as es:
        g.es = es
        for s in ["sD", "sD2", "sD3", "sV", "sS", "sT", "sG", "sE", "sP", "sQ"]:
            setattr(g, s, SemT(nc, es, s))
        g.dstreams = {}
        for name, n in [("dx", 3), ("dwa", 2), ("dwqo", 2), ("dwp", 2),
                        ("dmisc", 2), ("dwqa", 1), ("dw13", 2), ("dw23", 4),
                        ("dw2qo", 2), ("dwpf", 2), ("dw15", 2), ("dh3o", 2),
                        ("dw2q7", 1), ("dh3i", 4), ("dx8", 1), ("dout", 2)]:
            g.dstreams[name] = DSlots(g, name, n)
            setattr(g, name, g.dstreams[name])

        Lf = lambda *a, **k: es.enter_context(nc.sbuf_tensor(*a, side="left", **k))
        g.Lf = Lf
        g.identf = Lf("identf", [128, 128], F32)
        g.identb = Lf("identb", [128, 128], BF16)
        g.trinb = Lf("trinb", [128, 128], BF16)
        g.onescol = Lf("onescol", [128, 1], F32)
        g.onesrow = Lf("onesrow", [1, 128], F32)
        g.pcol = Lf("pcol", [128, 1], F32)
        g.cwb = Lf("cwb", [128, NJ, 3], F32)
        g.cbb = Lf("cbb", [128, NJ], F32)
        g.clipcol = Lf("clipcol", [128, NT], F32)
        g.s1col = Lf("s1col", [128, NT], F32)
        g.sm = Lf("sm", [128, 32], F32)
        g.row = Lf("rowscr", [1, 256], F32)
        g.srow = Lf("srow", [1, 32], F32)
        g.bc = Lf("bc", [128, 24], F32)
        g.sabs = Lf("sabs", [128, 144], F32)
        g.cbrow = Lf("cbrow", [128, 64], F32)

        _const_phase(g)

        def mkR(stack):
            return lambda *a, **k: stack.enter_context(
                nc.sbuf_tensor(*a, side="right", **k))

        SEQ = ["p0", "p1", "p2", "p3", "p4", "p5", "p7", "p8"]
        lim = SEQ.index(stop_after) + 1 if stop_after else len(SEQ)

        def on(ph):
            return SEQ.index(ph) < lim

        aTs = contextlib.ExitStack()
        g.aT = aTs.enter_context(
            nc.sbuf_tensor("aT", [128, NCC, T], BF16, side="right"))
        with contextlib.ExitStack() as ps03:
            with contextlib.ExitStack() as qkvs:
                Lq = lambda *a, **k: qkvs.enter_context(
                    nc.sbuf_tensor(*a, side="left", **k))
                with contextlib.ExitStack() as p01r:
                    g.xnT = p01r.enter_context(
                        nc.sbuf_tensor("xnT", [128, NCC, T], BF16, side="right"))
                    if on("p0"):
                        with contextlib.ExitStack() as s_:
                            _p0(g, mkR(s_))
                    if on("p1"):
                        g.QT = Lq("QT", [128, NCC, T], BF16)
                        g.KT = Lq("KT", [128, NCC, T], BF16)
                        g.Vb = Lq("Vb", [128, NT, H * 66], BF16)
                        with contextlib.ExitStack() as s_:
                            _p1(g, mkR(s_))
                    if on("p2"):
                        with contextlib.ExitStack() as s_:
                            _p2(g, mkR(s_))
                if on("p3"):
                    g.PT = Lq("PT", [128, 28, 512], BF16)
                    g.YT = ps03.enter_context(
                        nc.sbuf_tensor("YT", [128, NCC, T], BF16, side="right"))
                    with contextlib.ExitStack() as s_:
                        _p3(g, mkR(s_))
            if on("p4"):
                g.attnT = Lf("attnT", [128, NCC, T], F32)
                with contextlib.ExitStack() as s_:
                    _p4(g, mkR(s_))
        if on("p5"):
            g.s2b = Lf("s2b", [128, T], F32)
            g.inv2b = Lf("inv2b", [128, T], F32)
            g.inv1b = Lf("inv1b", [128, T], F32)
            with contextlib.ExitStack() as s_:
                _p5(g, mkR(s_))
        aTs.close()
        if on("p7"):
            with contextlib.ExitStack() as s_:
                _p7(g, mkR(s_))
        if on("p8"):
            with contextlib.ExitStack() as s_:
                _p8(g, mkR(s_))

    return nc


def _const_phase(g):
    nc = g.nc
    with nc.Block("const") as blk:
        @blk.gpsimd
        def _(gp):
            g.sG.inc(gp.iota(g.identf[:, :], [[1, 128]], channel_multiplier=0,
                             allow_small_or_imprecise_dtypes=True), 0) if False else \
                gp.iota(g.identf[:, :], [[1, 128]], channel_multiplier=0,
                        allow_small_or_imprecise_dtypes=True)
            g.sG.inc(gp.iota(g.pcol[:, :], [[0, 1]], channel_multiplier=1,
                             allow_small_or_imprecise_dtypes=True), 2)

        @blk.vector
        def _(v):
            g.sG.wait(v)
            v.tensor_scalar(g.identb[:, :], g.identf[:, :], g.pcol[:, :], None,
                            Alu.is_equal)
            v.tensor_scalar(g.trinb[:, :], g.identf[:, :], g.pcol[:, :], NEG,
                            Alu.is_lt, Alu.mult)
            v.tensor_scalar(g.identf[:, :], g.identf[:, :], g.pcol[:, :], None,
                            Alu.is_equal)
            v.memset(g.onescol[:, :], 1.0)
            v.memset(g.onesrow[:, :], 1.0)
            v.memset(g.sabs[:, :], 0.0)
            v.memset(g.sm[:, :], 0.0)
            v.memset(g.row[:, :], 0.0)
            v.memset(g.srow[:, :], 0.0)
            v.memset(g.bc[:, :], 0.0)
            v.memset(g.cbrow[:, :], 0.0)
            v.memset(g.clipcol[:, :], 0.0)
            v.memset(g.s1col[:, :], 0.0)
def _wfin(g, blk, cols, nelem, clip_lo, slot, name):
    nc = g.nc

    @blk.vector
    def _(v):
        g.sV.inc(v.reduce_sum(g.sm[:, 0:1], cols, AX.X))
    vm = g.sV.n

    with nc.psum_tensor("pw1_" + name, [1, 512], F32) as p1, \
         nc.psum_tensor("pw2_" + name, [128, 512], F32) as pb:
        @blk.tensor
        def _(t):
            g.sV.wait_at(t, vm)
            g.sT.inc(t.matmul(p1[:, 0:1], g.onescol[:, :], g.sm[:, 0:1], start=True, stop=True))
        tm = g.sT.n

        @blk.vector
        def _(v):
            g.sT.wait_at(v, tm)
            ch = chained(g, v)
            ch.tensor_scalar(g.srow[0:1, slot:slot + 1], p1[:, 0:1], 1.0 / nelem,
                             clip_lo, Alu.mult, Alu.max)
            g.sV.inc(v.reciprocal(g.srow[0:1, slot + 1:slot + 2],
                                   g.srow[0:1, slot:slot + 1]))
        vm2 = g.sV.n

        @blk.tensor
        def _(t):
            g.sV.wait_at(t, vm2)
            g.sT.inc(t.matmul(pb[:, 0:2], g.onesrow[:, :], g.srow[0:1, slot:slot + 2],
                     start=True, stop=True))
        tm2 = g.sT.n

        @blk.vector
        def _(v):
            g.sT.wait_at(v, tm2)
            g.sV.inc(v.tensor_copy(g.bc[:, slot:slot + 2], pb[:, 0:2]))


def _scalar_bcast(g, blk, src_slot, dst_slot, name):
    nc = g.nc
    vm = g.sV.n
    with nc.psum_tensor("pb_" + name, [128, 512], F32) as pb:
        @blk.tensor
        def _(t):
            g.sV.wait_at(t, vm)
            g.sT.inc(t.matmul(pb[:, 0:1], g.onesrow[:, :], g.srow[0:1, src_slot:src_slot + 1],
                     start=True, stop=True))
        tm = g.sT.n

        @blk.vector
        def _(v):
            g.sT.wait_at(v, tm)
            g.sV.inc(v.tensor_copy(g.bc[:, dst_slot:dst_slot + 1], pb[:, 0:1]))


def _dbg_dump2(g, out_ap, in_ap, name):
    if not g.debug:
        return
    nc = g.nc
    with nc.Block("dbg_" + name) as blk:
        @blk.gpsimd
        def _(gp):
            g.sD3.inc(gp.dma_start(out=out_ap, in_=in_ap), 16)
            g.sD3.wait(gp)


def _dbg_chunks(g, dram, sbuf3, nchunks, name):
    if not g.debug:
        return
    nc = g.nc
    with nc.Block("dbgc_" + name) as blk:
        @blk.gpsimd
        def _(gp):
            for n in range(nchunks):
                g.sD3.inc(gp.dma_start(
                    out=dram.rearrange("(n p) w -> n p w", p=128)[n],
                    in_=sbuf3[:, n, :]), 16)
            g.sD3.wait(gp)


def _p0(g, R):
    g.d_wqo_last = [0, 0]
    nc = g.nc
    xring = R("xring", [128, 3, C], F32)
    xnb = R("xnb", [128, 2, C], BF16)
    ai = R("ai", [128, C], I32)
    ab = R("ab", [128, 2, C], BF16)
    avF = R("avF", [128, C], F32)
    acf = R("acf", [128, C], F32)
    ap_ = R("ap_", [128, C], F32)
    an_ = R("an_", [128, C], F32)
    wsb = R("wsb", [128, 2, NATT], F32)
    wpos = R("wpos", [128, NATT], F32)
    wneg = R("wneg", [128, NATT], F32)
    wqab = R("wqab", [128, 2, NATT], BF16)
    wstr = R("wstr", [128, 2, C], F32)
    sink3 = R("sink3", [128, NATT], F32)
    SMa = R("SMa", [128, 16], F32)
    SMb = R("SMb", [128, 16], F32)
    AMX = R("AMX", [128, 16], F32)
    rcol = R("rcol", [128, 16], F32)
    rt0 = R("rt0", [128, 16], F32)
    mrcol = R("mrcol", [128, 16], F32)
    tm16 = R("tm16", [128, 16], F32)
    rs = R("rs", [128, 16], F32)
    mrs = R("mrs", [128, 16], F32)

    x_r = g.x_d.rearrange("(n p) c -> n p c", p=128)
    wa_r = g.wattnT_d.rearrange("(n p) o -> n p o", p=128)
    wp_r = g.wprojT_d.rearrange("(n p) o -> n p o", p=128)
    w2_r = g.w2T_d.rearrange("(n p) o -> n p o", p=128)

    d_x = [0] * 48
    v_cons = [0] * 48
    d_wa = [0] * NCC
    s_wa = [0] * NCC
    d_w2a = [0] * NJ
    s_w2a = [0] * NJ
    t_x = [0] * NT
    t_a = [0] * NT
    s_xe = [0] * NT
    s_ae = [0] * NT

    with nc.Block("p0") as blk, \
         nc.psum_tensor("pxa", [128, 512], F32) as px0, \
         nc.psum_tensor("pxb", [128, 512], F32) as px1, \
         nc.psum_tensor("paa", [128, 512], F32) as pa0, \
         nc.psum_tensor("pab", [128, 512], F32) as pa1:
        pxs = [px0, px1]
        pas = [pa0, pa1]

        @blk.sync
        def _(sy):
            g.dmisc.inc(sy.dma_start(out=g.cwb[:, :, :],
                                     in_=g.convw_d.rearrange("n p k -> p n k"),
                                     allow_slow_non_contiguous=True), 0)
            g.dmisc.inc(sy.dma_start(out=g.cbb[:, :],
                                     in_=g.convb_d.rearrange("n p -> p n"),
                                     allow_slow_non_contiguous=True), 1)
            for L in range(3):
                d_x[L] = g.dx.inc(sy.dma_start(out=xring[:, L % 3, :],
                                               in_=x_r[L]), L % 3)

        @blk.scalar
        def _(sc):
            for cc in range(2):
                d_wa[cc] = g.dwa.inc(sc.dma_start(out=wsb[:, cc, :],
                                                  in_=wa_r[cc]), cc)

        def prefetch(L):
            @blk.sync
            def _(sy, L=L):
                if L + 3 < 48:
                    g.sV.wait_at(sy, v_cons[L])
                    d_x[L + 3] = g.dx.inc(sy.dma_start(
                        out=xring[:, (L + 3) % 3, :], in_=x_r[(L + 3) % 16]),
                        (L + 3) % 3)

        for i in range(NT):
            @blk.vector
            def _(v, i=i):
                g.dx.wait(v, d_x[i])
                v.reduce_sum(SMa[:, i:i + 1], xring[:, i % 3, :], AX.X)
                g.sV.inc(v.scalar_tensor_tensor(sink3[:, 0:C], xring[:, i % 3, :],
                                                1.0, xring[:, i % 3, :],
                                                Alu.mult, Alu.mult,
                                                accum_out=SMb[:, i:i + 1]))
                v_cons[i] = g.sV.n
            prefetch(i)

            if i % 2 == 1 and (i - 1) // 2 < NCC:
                cc = (i - 1) // 2

                @blk.scalar
                def _(s, cc=cc):
                    g.dwa.wait(s, d_wa[cc])
                    g.sS.inc(s.activation(sink3[:, :], wsb[:, cc % 2, :], AF.Abs,
                                          accum_out=g.sabs[:, cc:cc + 1]))
                    s_wa[cc] = g.sS.n
                    if cc + 2 < NCC:
                        d_wa[cc + 2] = g.dwa.inc(
                            s.dma_start(out=wsb[:, cc % 2, :], in_=wa_r[cc + 2]),
                            cc % 2)

        @blk.vector
        def _(v, w=v_cons[NT - 1]):
            g.sV.wait_at(v, w)
            ch = chained(g, v)
            ch.tensor_scalar(SMa[:, :], SMa[:, :], 1.0 / C, None, Alu.mult)
            ch.tensor_scalar(SMb[:, :], SMb[:, :], 1.0 / C, None, Alu.mult)
            ch.tensor_tensor(tm16[:, :], SMa[:, :], SMa[:, :], Alu.mult)
            ch.tensor_tensor(SMb[:, :], SMb[:, :], tm16[:, :], Alu.subtract)
            ch.tensor_scalar(SMb[:, :], SMb[:, :], EPS_LN, None, Alu.add)
            g.sV.inc(v.reciprocal(rcol[:, :], SMb[:, :]))
        v_b1 = g.sV.n

        @blk.scalar
        def _(s):
            g.sV.wait_at(s, v_b1)
            g.sS.inc(s.activation(rt0[:, :], rcol[:, :], AF.Sqrt))
        s_b = g.sS.n

        @blk.vector
        def _(v):
            g.sS.wait_at(v, s_b)
            ch = chained(g, v)
            ch.reciprocal(tm16[:, :], rt0[:, :])
            ch.tensor_tensor(tm16[:, :], rcol[:, :], tm16[:, :], Alu.mult)
            ch.tensor_tensor(tm16[:, :], rt0[:, :], tm16[:, :], Alu.add)
            ch.tensor_scalar(rcol[:, :], tm16[:, :], 0.5, None, Alu.mult)
            ch.tensor_tensor(mrcol[:, :], SMa[:, :], rcol[:, :], Alu.mult)
            g.sV.inc(v.tensor_scalar(mrcol[:, :], mrcol[:, :], -1.0, None,
                                     Alu.mult))
        v_b2 = g.sV.n

        @blk.scalar
        def _(s):
            for m in range(2):
                g.sS.wait_at(s, s_wa[NCC - 2 + m])
                d_w2a[m] = g.dw23.inc(s.dma_start(out=wsb[:, m, 0:C],
                                                  in_=w2_r[m]), m)

        def w2chunk(m):
            @blk.scalar
            def _(s, m=m):
                g.dw23.wait(s, d_w2a[m])
                g.sS.inc(s.activation(sink3[:, 0:C], wsb[:, m % 2, 0:C],
                                      AF.Abs,
                                      accum_out=g.sabs[:, 48 + m:49 + m]))
                s_w2a[m] = g.sS.n
                if m + 2 < NJ:
                    d_w2a[m + 2] = g.dw23.inc(s.dma_start(
                        out=wsb[:, m % 2, 0:C], in_=w2_r[m + 2]), m % 2)

        for i in range(NT):
            L = 16 + i

            @blk.vector
            def _(v, i=i, L=L):
                g.dx.wait(v, d_x[L])
                if i == 0:
                    g.sV.wait_at(v, v_b2)
                if i >= 2:
                    g.sT.wait_at(v, t_x[i - 2])
                v.tensor_scalar(xnb[:, i % 2, :], xring[:, L % 3, :],
                                rcol[:, i:i + 1], mrcol[:, i:i + 1],
                                Alu.mult, Alu.add)
                g.sV.inc(v.reduce_max(AMX[:, i:i + 1], xnb[:, i % 2, :], AX.X,
                                      apply_absolute_value=True))
                v_cons[L] = g.sV.n
            v_c = g.sV.n
            prefetch(L)

            @blk.tensor
            def _(t, i=i, v_c=v_c):
                g.sV.wait_at(t, v_c)
                if i >= 1:
                    g.sS.wait_at(t, s_xe[i - 1])
                for half in range(2):
                    for q in range(4):
                        cc = 4 * half + q
                        ins = t.matmul(pxs[half][:, 128 * q:128 * q + 128],
                                       xnb[:, i % 2, 128 * cc:128 * cc + 128],
                                       g.identb[:, :],
                                       start=(q == 0), stop=(q == 3))
                g.sT.inc(ins)
                t_x[i] = g.sT.n

            @blk.scalar
            def _(s, i=i):
                g.sT.wait_at(s, t_x[i])
                for half in range(2):
                    ins = s.activation(
                        g.xnT[:, 4 * half:4 * half + 4, 128 * i:128 * i + 128],
                        pxs[half][:, :].rearrange("p (n c) -> p n c", c=128),
                        AF.Copy)
                g.sS.inc(ins)
                s_xe[i] = g.sS.n
            w2chunk(i)

        @blk.vector
        def _(v, w=v_cons[31]):
            g.sV.wait_at(v, w)
            ch = chained(g, v)
            ch.tensor_scalar(g.clipcol[:, 0:NT], AMX[:, :], 1e-5, None, Alu.max)
            ch.reciprocal(tm16[:, :], g.clipcol[:, 0:NT])
            ch.tensor_scalar(g.s1col[:, 0:NT], tm16[:, :], 127.0, None,
                             Alu.mult)
            ch.tensor_tensor(rs[:, :], rcol[:, :], g.s1col[:, 0:NT], Alu.mult)
            ch.tensor_tensor(mrs[:, :], mrcol[:, :], g.s1col[:, 0:NT], Alu.mult)
            g.sV.inc(v.tensor_copy(tm16[:, :], rs[:, :]))
        v_d = g.sV.n

        for i in range(NT):
            L = 32 + i

            @blk.vector
            def _(v, i=i, L=L):
                g.dx.wait(v, d_x[L])
                if i == 0:
                    g.sV.wait_at(v, v_d)
                if i >= 2:
                    g.sT.wait_at(v, t_a[i - 2])
                v.tensor_scalar(avF[:, :], xring[:, L % 3, :],
                                rs[:, i:i + 1], mrs[:, i:i + 1],
                                Alu.mult, Alu.add)
                v.tensor_copy(ai[:, :], avF[:, :])
                v.tensor_copy(acf[:, :], ai[:, :])
                v.scalar_tensor_tensor(ap_[:, :], avF[:, :], -0.5, acf[:, :],
                                       Alu.add, Alu.is_ge)
                v.scalar_tensor_tensor(an_[:, :], avF[:, :], 0.5, acf[:, :],
                                       Alu.add, Alu.is_le)
                v.tensor_tensor(acf[:, :], acf[:, :], ap_[:, :], Alu.add)
                g.sV.inc(v.tensor_tensor(ab[:, i % 2, :], acf[:, :], an_[:, :],
                                         Alu.subtract))
                v_cons[L] = g.sV.n
            v_e = g.sV.n
            prefetch(L)

            @blk.tensor
            def _(t, i=i, v_e=v_e):
                g.sV.wait_at(t, v_e)
                if i >= 1:
                    g.sS.wait_at(t, s_ae[i - 1])
                for half in range(2):
                    for q in range(4):
                        cc = 4 * half + q
                        ins = t.matmul(pas[half][:, 128 * q:128 * q + 128],
                                       ab[:, i % 2, 128 * cc:128 * cc + 128],
                                       g.identb[:, :],
                                       start=(q == 0), stop=(q == 3))
                g.sT.inc(ins)
                t_a[i] = g.sT.n

            @blk.scalar
            def _(s, i=i):
                g.sT.wait_at(s, t_a[i])
                for half in range(2):
                    ins = s.activation(
                        g.aT[:, 4 * half:4 * half + 4, 128 * i:128 * i + 128],
                        pas[half][:, :].rearrange("p (n c) -> p n c", c=128),
                        AF.Copy)
                g.sS.inc(ins)
                s_ae[i] = g.sS.n
            w2chunk(16 + i)

        _wfin(g, blk, g.sabs[:, 0:NCC], float(C) * NATT, 1e-8, 0, "wa")

        @blk.vector
        def _(v, w=None):
            v.wait_ge(g.sS.h, s_w2a[NJ - 1])
        _wfin(g, blk, g.sabs[:, 48:48 + NJ], float(HID) * C, 1e-5, 6, "w2")
        g.w2fin_mark = g.sV.n

        @blk.vector
        def _(v):
            g.sQ.wait(v)
            ch = chained(g, v)
            ch.tensor_scalar(g.srow[0:1, 20:21], g.srow[0:1, 0:1], 0.5, None,
                             Alu.mult)
            ch.tensor_scalar(g.srow[0:1, 21:22], g.srow[0:1, 0:1], 1.5, None,
                             Alu.mult)
            ch.tensor_scalar(g.srow[0:1, 22:23], g.srow[0:1, 0:1], -0.5, None,
                             Alu.mult)
            g.sV.inc(v.tensor_scalar(g.srow[0:1, 23:24], g.srow[0:1, 0:1],
                                     -1.5, None, Alu.mult))
        v_th = g.sV.n

        with nc.psum_tensor("pth", [128, 512], F32) as pth:
            @blk.tensor
            def _(t):
                g.sV.wait_at(t, v_th)
                g.sT.inc(t.matmul(pth[:, 0:4], g.onesrow[:, :],
                                  g.srow[0:1, 20:24], start=True, stop=True))
            t_th = g.sT.n

            @blk.vector
            def _(v):
                g.sT.wait_at(v, t_th)
                v.tensor_copy(g.bc[:, 13:16], pth[:, 0:3])
                g.sV.inc(v.tensor_copy(g.bc[:, 9:10], pth[:, 3:4]))
        v_thb = g.sV.n

        d_waq = [0] * NCC
        v_wq = [0] * NCC
        d_wqo = [0] * NCC
        for cc in range(NCC):
            @blk.scalar
            def _(s, cc=cc):
                if cc < 2:
                    g.sS.wait_at(s, s_w2a[NJ - 2 + cc])
                else:
                    g.sV.wait_at(s, v_wq[cc - 2])
                d_waq[cc] = g.dwa.inc(
                    s.dma_start(out=wsb[:, cc % 2, :], in_=wa_r[cc]), cc % 2)

            @blk.vector
            def _(v, cc=cc):
                g.dwa.wait(v, d_waq[cc])
                if cc == 0:
                    g.sV.wait_at(v, v_thb)
                if cc >= 2:
                    g.dwqo.wait(v, d_wqo[cc - 2])
                wt = wsb[:, cc % 2, :]
                v.tensor_scalar(wpos[:, :], wt, g.bc[:, 13:14], None, Alu.is_ge)
                v.scalar_tensor_tensor(wpos[:, :], wt, g.bc[:, 14:15],
                                       wpos[:, :], Alu.is_ge, Alu.add)
                v.tensor_scalar(wneg[:, :], wt, g.bc[:, 15:16], None, Alu.is_le)
                v.scalar_tensor_tensor(wneg[:, :], wt, g.bc[:, 9:10],
                                       wneg[:, :], Alu.is_le, Alu.add)
                g.sV.inc(v.tensor_tensor(wqab[:, cc % 2, :], wpos[:, :],
                                         wneg[:, :], Alu.subtract))
                v_wq[cc] = g.sV.n

            @blk.scalar
            def _(s, cc=cc):
                g.sV.wait_at(s, v_wq[cc])
                d_wqo[cc] = g.dwqo.inc(
                    s.dma_start(out=g.wqascr[cc], in_=wqab[:, cc % 2, :]), cc % 2)
                g.d_wqo_last[cc % 2] = d_wqo[cc]

        d_wp = [0] * NCC
        s_wp = [0] * NCC
        for cc in range(NCC):
            @blk.scalar
            def _(s, cc=cc):
                if cc >= 2:
                    g.sS.wait_at(s, s_wp[cc - 2])
                d_wp[cc] = g.dwp.inc(s.dma_start(out=wstr[:, cc % 2, :],
                                                 in_=wp_r[cc]), cc % 2)

            @blk.scalar
            def _(s, cc=cc):
                g.dwp.wait(s, d_wp[cc])
                g.sS.inc(s.activation(sink3[:, 0:C], wstr[:, cc % 2, :], AF.Abs,
                                      accum_out=g.sabs[:, 8 + cc:9 + cc]))
                s_wp[cc] = g.sS.n

    if g.debug:
        _dbg_chunks(g, g.dbg["dbg_xnT"], g.xnT, NCC, "xnT")
        _dbg_chunks(g, g.dbg["dbg_aT"], g.aT, NCC, "aT")
def _p1(g, R):
    nc = g.nc
    d_wqo_last = g.d_wqo_last
    wqa = R("wqa", [128, NCC, 1536], BF16)
    wstr1 = R("wstr1", [128, 4, 512], F32)
    w2p1 = R("w2p1", [128, 512], F32)
    w2n1 = R("w2n1", [128, 512], F32)
    w2qb1 = R("w2qb1", [128, 2, 512], BF16)
    w2_r = g.w2T_d.rearrange("(n p) o -> n p o", p=128)

    d_w2q = [0] * (2 * NJ)
    v_w2q = [0] * (2 * NJ)
    d_w2qo = [0, 0]
    wsteps = []

    def mk_dma(k):
        def dma(blk):
            @blk.scalar
            def _(s):
                if k >= 4:
                    g.sV.wait_at(s, v_w2q[k - 4])
                d_w2q[k] = g.dw23.inc(s.dma_start(
                    out=wstr1[:, k % 4, :],
                    in_=w2_r[k // 2][:, 512 * (k % 2):512 * (k % 2) + 512]),
                    k % 4)
        return dma

    def mk_w2q(k, jj, hf):
        def quant(blk):
            @blk.vector
            def _(v):
                g.dw23.wait(v, d_w2q[k])
                g.dw2qo.wait(v, d_w2qo[k % 2])
                v.tensor_scalar(w2p1[:, :], wstr1[:, k % 4, :], g.bc[:, 7:8],
                                0.5, Alu.mult, Alu.is_ge)
                v.tensor_scalar(w2n1[:, :], wstr1[:, k % 4, :], g.bc[:, 7:8],
                                -0.5, Alu.mult, Alu.is_le)
                g.sV.inc(v.tensor_tensor(w2qb1[:, k % 2, :], w2p1[:, :],
                                         w2n1[:, :], Alu.subtract))
                v_w2q[k] = g.sV.n
        def out(blk):
            @blk.scalar
            def _(s):
                g.sV.wait_at(s, v_w2q[k])
                d_w2qo[k % 2] = g.dw2qo.inc(s.dma_start(
                    out=g.w2qscr[jj][:, 512 * hf:512 * hf + 512],
                    in_=w2qb1[:, k % 2, :]), k % 2)
        return [quant, out]

    for k in range(4):
        wsteps.append(mk_dma(k))
    kk = 0
    for jj in range(NJ):
        for hf in range(2):
            q, o = mk_w2q(kk, jj, hf)
            wsteps.append(q)
            wsteps.append(o)
            if kk + 4 < 2 * NJ:
                wsteps.append(mk_dma(kk + 4))
            kk += 1

    w1_r = g.w1cm_d
    d_w1h = [0] * (2 * NJ)
    s_w1h = [0] * (2 * NJ)

    def mk_w1h(k):
        jj, hf = k // 2, k % 2

        def dma(blk):
            @blk.scalar
            def _(s):
                if k >= 4:
                    g.sV.wait_at(s, s_w1h[k - 4])
                else:
                    g.sV.wait_at(s, v_w2q[2 * NJ - 4 + k])
                d_w1h[k] = g.dw13.inc(s.dma_start(
                    out=wstr1[:, k % 4, :].rearrange("p (a j) -> p a j", a=4),
                    in_=w1_r[jj].rearrange("(a p) j -> p a j", p=128)
                        [:, 4 * hf:4 * hf + 4, :]), k % 2)

        def red(blk):
            @blk.vector
            def _(v):
                g.dw13.wait(v, d_w1h[k])
                g.sV.inc(v.reduce_sum(g.sabs[:, 80 + k:81 + k],
                                      wstr1[:, k % 4, :], AX.X,
                                      apply_absolute_value=True))
                s_w1h[k] = g.sV.n
        return [dma, red]

    for k in range(2 * NJ):
        wsteps.extend(mk_w1h(k))
    wsi = [0]

    def emit_w(blk, k):
        n = 0
        while n < k and wsi[0] < len(wsteps):
            wsteps[wsi[0]](blk)
            wsi[0] += 1
            n += 1

    with nc.Block("p1") as blk:
        @blk.vector
        def _(v):
            v.memset(g.Vb[:, :, :], 0.0)
            g.sV.inc(v.memset(
                g.Vb[:, :, :].rearrange("p n (h w) -> p n h w", w=66)[:, :, :, 64:65],
                1.0))

        e_qk = {}
        for half in range(2):
            @blk.sync
            def _(sy, half=half):
                if half == 1:
                    g.sE.wait_at(sy, e_qk[11])
                g.dwqo.wait(sy, d_wqo_last[0])
                g.dwqo.wait(sy, d_wqo_last[1])
                g.dwqa.inc(sy.dma_start(
                    out=wqa[:, :, :],
                    in_=g.wqascr[:, :, 1536 * half:1536 * half + 1536]
                        .rearrange("n p o -> p n o")), 0)
            d_w = (0, g.dwqa.sems[0].n)

            with nc.psum_tensor("pqk0_%d" % half, [128, T], F32) as q0, \
                 nc.psum_tensor("pqk1_%d" % half, [128, T], F32) as q1:
                pq = [q0, q1]
                ocs = list(range(12)) if half == 0 else list(range(12, 16))
                for idx, oc in enumerate(ocs):
                    off = 128 * oc - 1536 * half

                    @blk.tensor
                    def _(t, oc=oc, off=off, d_w=d_w, idx=idx):
                        g.dwqa.wait(t, d_w)
                        if idx >= 2:
                            g.sE.wait_at(t, e_qk[oc - 2])
                        for cc in range(NCC):
                            lhsT = wqa[:, cc, off:off + 128]
                            for th in range(4):
                                ins = t.matmul(pq[oc % 2][:, 512 * th:512 * th + 512],
                                         lhsT,
                                         g.xnT[:, cc, 512 * th:512 * th + 512],
                                         start=(cc == 0), stop=(cc == NCC - 1))
                        g.sT.inc(ins)
                    tm = g.sT.n

                    @blk.vector
                    def _(v, oc=oc, tm=tm):
                        g.sT.wait_at(v, tm)
                        dst = (g.QT[:, oc, :] if oc < 8 else g.KT[:, oc - 8, :])
                        g.sE.inc(v.tensor_copy(dst, pq[oc % 2][:, :]))
                        e_qk[oc] = g.sE.n
                    emit_w(blk, 11)

            if half == 1:
                with nc.psum_tensor("pv0", [128, 1024], F32) as v0, \
                     nc.psum_tensor("pv1", [128, 1024], F32) as v1:
                    pv = [v0, v1]
                    e_v = [0] * NT
                    for tc in range(NT):
                        @blk.tensor
                        def _(t, tc=tc):
                            if tc == 0:
                                g.sE.wait_at(t, e_qk[15])
                            if tc >= 2:
                                g.sE.wait_at(t, e_v[tc - 2])
                            for cc in range(NCC):
                                lhsT = g.xnT[:, cc, 128 * tc:128 * tc + 128]
                                for vh in range(2):
                                    rhs = wqa[:, cc, 512 + 512 * vh:1024 + 512 * vh]
                                    ins = t.matmul(pv[tc % 2][:, 512 * vh:512 * vh + 512],
                                             lhsT, rhs,
                                             start=(cc == 0), stop=(cc == NCC - 1))
                            g.sT.inc(ins)
                        tm = g.sT.n

                        @blk.vector
                        def _(v, tc=tc, tm=tm):
                            g.sT.wait_at(v, tm)
                            g.sE.inc(v.tensor_copy(
                                g.Vb[:, tc, :].rearrange("p (h w) -> p h w",
                                                         w=66)[:, :, 0:64],
                                pv[tc % 2][:, :].rearrange("p (h d) -> p h d",
                                                           d=64)))
                            e_v[tc] = g.sE.n
                        emit_w(blk, 11)

        emit_w(blk, len(wsteps))

    if g.debug:
        _dbg_chunks(g, g.dbg["dbg_QT"], g.QT, NCC, "QT")
        _dbg_chunks(g, g.dbg["dbg_KT"], g.KT, NCC, "KT")
        _dbg_chunks(g, g.dbg["dbg_Vb"], g.Vb, NT, "Vb")
        _dbg_dump2(g, g.dbg["dbg_scales"][:, 0:16], g.bc[:, 0:16], "scales")


def _p2(g, R):
    nc = g.nc
    with nc.Block("p2") as blk:
        @blk.vector
        def _(v):
            ch = chained(g, v)
            ch.tensor_tensor(g.srow[0:1, 16:17], g.srow[0:1, 0:1],
                             g.srow[0:1, 0:1], Alu.mult)
            g.sV.inc(v.tensor_scalar(g.srow[0:1, 16:17], g.srow[0:1, 16:17],
                                      1.0 / 8.0, None, Alu.mult))
        _scalar_bcast(g, blk, 16, 8, "alpha")
def _p3(g, R):
    nc = g.nc
    recrow = R("recrow", [1, 512], F32)
    rb = R("rb", [128, 512], F32)

    sa = contextlib.ExitStack()
    Ra = lambda *a, **k: sa.enter_context(nc.sbuf_tensor(*a, side="right", **k))
    cmaxw = Ra("cmaxw", [128, 1024], F32)
    cmax256 = Ra("cmax256", [128, 256], F32)
    gma = Ra("gma", [128, 64], F32)
    grow = Ra("grow", [64, 1], F32)
    with nc.Block("p3a") as blk, \
         nc.psum_tensor("pX3", [128, 2048], F32) as pX, \
         nc.psum_tensor("pC3", [128, 512], F32) as pC:
        v_bank = [0, 0, 0, 0]

        @blk.vector
        def _(v):
            g.sV.inc(v.memset(cmaxw[:, :], NEG))

        for h in range(H):
            hc, hh = h // 2, h % 2
            r0, r1 = 64 * hh, 64 * hh + 64
            for qi in range(NT):
                width = 128 * (qi + 1)
                nk = (width + 511) // 512
                t_chunk = [0] * nk

                @blk.tensor
                def _(t, h=h, qi=qi, width=width, nk=nk, hc=hc, r0=r0, r1=r1,
                      vb=tuple(v_bank)):
                    for ck in range(nk):
                        w = min(512, width - 512 * ck)
                        g.sV.wait_at(t, vb[ck])
                        g.sT.inc(t.matmul(
                            pX[:, 512 * ck:512 * ck + w],
                            g.QT[r0:r1, hc, 128 * qi:128 * qi + 128],
                            g.KT[r0:r1, hc, 512 * ck:512 * ck + w],
                            start=True, stop=True))
                        t_chunk[ck] = g.sT.n

                @blk.vector
                def _(v, h=h, qi=qi, width=width, nk=nk, t_chunk=t_chunk,
                      last=(h == H - 1 and qi == NT - 1)):
                    for ck in range(nk):
                        w = min(512, width - 512 * ck)
                        g.sT.wait_at(v, t_chunk[ck])
                        ins = v.reduce_max(
                            cmaxw[:, 64 * h + 4 * qi + ck:64 * h + 4 * qi + ck + 1],
                            pX[:, 512 * ck:512 * ck + w], AX.X)
                        if last and ck == nk - 1:
                            g.sQ.inc(ins)
                        else:
                            g.sV.inc(ins)
                            v_bank[ck] = g.sV.n



        @blk.vector
        def _(v):
            g.sQ.wait(v)
            ch = chained(g, v)
            ch.reduce_max(cmax256[:, :],
                          cmaxw[:, :].rearrange("p (a f) -> p a f", f=4), AX.X)
            g.sV.inc(v.reduce_max(
                gma[:, :], cmax256[:, :].rearrange("p (a f) -> p a f", f=4),
                AX.X))
        vm1 = g.sV.n

        @blk.tensor
        def _(t, vm1=vm1):
            g.sV.wait_at(t, vm1)
            g.sT.inc(t.matmul(pC[0:64, 0:128], gma[:, :], g.identf[:, :],
                              start=True, stop=True))
        tm1 = g.sT.n

        @blk.vector
        def _(v, tm1=tm1):
            g.sT.wait_at(v, tm1)
            ch = chained(g, v)
            ch.reduce_max(grow[:, :], pC[0:64, 0:128], AX.X)
            g.sV.inc(v.tensor_scalar(grow[:, :], grow[:, :], g.bc[0:64, 8:9],
                                     -1.0, Alu.mult, Alu.mult))
        vm2 = g.sV.n

        @blk.tensor
        def _(t, vm2=vm2):
            g.sV.wait_at(t, vm2)
            g.sT.inc(t.matmul(pC[0:1, 128:192], grow[:, :],
                              g.identf[0:64, 0:64], start=True, stop=True))
        tm2 = g.sT.n

        @blk.vector
        def _(v, tm2=tm2):
            g.sT.wait_at(v, tm2)
            g.sV.inc(v.tensor_copy(g.row[0:1, 0:64], pC[0:1, 128:192]))
        vm3 = g.sV.n

        @blk.tensor
        def _(t, vm3=vm3):
            g.sV.wait_at(t, vm3)
            g.sT.inc(t.matmul(pC[:, 256:320], g.onesrow[:, :],
                              g.row[0:1, 0:64], start=True, stop=True))
        tm3 = g.sT.n

        @blk.vector
        def _(v, tm3=tm3):
            g.sT.wait_at(v, tm3)
            g.sV.inc(v.tensor_copy(g.cbrow[:, 0:64], pC[:, 256:320]))

    sa.close()

    with nc.Block("p3b") as blk, \
         nc.psum_tensor("pS0", [128, 512], F32) as pS0, \
         nc.psum_tensor("pS1", [128, 512], F32) as pS1, \
         nc.psum_tensor("pS2", [128, 512], F32) as pS2, \
         nc.psum_tensor("pS3", [128, 512], F32) as pS3, \
         nc.psum_tensor("pS4", [128, 512], F32) as pS4, \
         nc.psum_tensor("pS5", [128, 512], F32) as pS5, \
         nc.psum_tensor("pY", [65, 512], F32) as pY, \
         nc.psum_tensor("pM", [128, 512], F32) as pM:
        pS = [pS0, pS1, pS2, pS3, pS4, pS5]
        ND = 6
        def ptidx(G, j):
            return (G % 2) * 12 + j if j < 12 else 12 + j
        pt_free = [0] * 28
        exp_slot_pred = [0] * ND
        splan_l = [g.sS.n]
        uglob_l = [0]
        e_y_l = [0]
        groups = [(h, gq) for h in range(H) for gq in range(4)]
        ctxs = []

        def emit_ST(G):
            h, gq = groups[G]
            hc, hh = h // 2, h % 2
            r0, r1 = 64 * hh, 64 * hh + 64
            nj = 4 * gq + 4
            units = [(j,) for j in range(nj)]
            ctx = dict(h=h, gq=gq, hc=hc, r0=r0, r1=r1, nj=nj, units=units,
                       t_s=[], s_e_plan=[], slots=[])
            work = []
            for unit in units:
                slot = uglob_l[0] % ND
                ctx["slots"].append(slot)
                uglob_l[0] += 1
                work.append((unit, slot, exp_slot_pred[slot]))
                splan_l[0] += 1
                ctx["s_e_plan"].append(splan_l[0])
                exp_slot_pred[slot] = splan_l[0]

            @blk.tensor
            def _(t, work=work, gq=gq, hc=hc, r0=r0, r1=r1, ctx=ctx):
                for unit, slot, esp in work:
                    if esp > 0:
                        t.wait_ge(g.sS.h, esp)
                    for k, j in enumerate(unit):
                        qo = max(0, 128 * j - 512 * gq)
                        width = 512 - qo
                        diag = j >= 4 * gq
                        ins = t.matmul(
                            pS[slot][:, 512 * k:512 * k + width],
                            g.KT[r0:r1, hc, 128 * j:128 * j + 128],
                            g.QT[r0:r1, hc, 512 * gq + qo:512 * gq + 512],
                            start=True, stop=not diag)
                        if diag:
                            ins = t.matmul(
                                pS[slot][:, 512 * k:512 * k + 128],
                                g.identb[:, :], g.trinb[:, :],
                                start=False, stop=True)
                    g.sT.inc(ins)
                    ctx["t_s"].append(g.sT.n)
            ctxs.append(ctx)

        def emit_EXP(G):
            ctx = ctxs[G]
            ptw = [tuple(pt_free[ptidx(G, j)] for j in unit)
                   for unit in ctx["units"]]

            @blk.scalar
            def _(s, ctx=ctx, ptw=ptw, G=G):
                h, gq = ctx["h"], ctx["gq"]
                for ui, unit in enumerate(ctx["units"]):
                    g.sT.wait_at(s, ctx["t_s"][ui])
                    for pv in ptw[ui]:
                        g.sT.wait_at(s, pv)
                    j0 = unit[0]
                    qo = max(0, 128 * j0 - 512 * gq)
                    g.sS.inc(s.activation(
                        g.PT[:, ptidx(G, j0), qo:512],
                        pS[ctx["slots"][ui]][:, 0:512 - qo], AF.Exp,
                        bias=g.cbrow[:, 4 * h + gq:4 * h + gq + 1],
                        scale=g.bc[:, 8:9]))
                    assert g.sS.n == ctx["s_e_plan"][ui]

        def emit_PV(G):
            ctx = ctxs[G]

            @blk.tensor
            def _(t, ctx=ctx, ey=e_y_l[0], G=G):
                h, gq, nj = ctx["h"], ctx["gq"], ctx["nj"]
                for ui, unit in enumerate(ctx["units"]):
                    t.wait_ge(g.sS.h, ctx["s_e_plan"][ui])
                    if ui == 0:
                        g.sE.wait_at(t, ey)
                    for j in unit:
                        qo = max(0, 128 * j - 512 * gq)
                        ins = t.matmul(pY[:, qo:512],
                                       g.Vb[:, j, 66 * h:66 * h + 65],
                                       g.PT[:, ptidx(G, j), qo:512],
                                       start=(j == 0), stop=(j == nj - 1))
                    g.sT.inc(ins)
                    for j in unit:
                        pt_free[ptidx(G, j)] = g.sT.n
            ctx["t_pv"] = g.sT.n

        def emit_TAIL(G):
            ctx = ctxs[G]

            @blk.vector
            def _(v, t_pv=ctx["t_pv"]):
                g.sT.wait_at(v, t_pv)
                g.sV.inc(v.reciprocal(recrow[0:1, :], pY[64:65, 0:512]))
            v_rec = g.sV.n

            @blk.tensor
            def _(t, v_rec=v_rec, ey=e_y_l[0]):
                g.sV.wait_at(t, v_rec)
                g.sE.wait_at(t, ey)
                g.sT.inc(t.matmul(pM[0:64, :], g.onesrow[0:1, 0:64],
                                  recrow[0:1, :], start=True, stop=True))
            t_r = g.sT.n

            @blk.vector
            def _(v, t_r=t_r, ctx=ctx):
                g.sT.wait_at(v, t_r)
                v.tensor_copy(rb[0:64, :], pM[0:64, :])
                g.sE.inc(v.tensor_tensor(
                    g.YT[ctx["r0"]:ctx["r1"], ctx["hc"],
                         512 * ctx["gq"]:512 * ctx["gq"] + 512],
                    pY[0:64, :], rb[0:64, :], Alu.mult))
            e_y_l[0] = g.sE.n

        emit_ST(0)
        emit_EXP(0)
        for G in range(1, len(groups)):
            emit_ST(G)
            emit_EXP(G)
            emit_PV(G - 1)
            emit_TAIL(G - 1)
        emit_PV(len(groups) - 1)
        emit_TAIL(len(groups) - 1)

    if g.debug:
        _dbg_chunks(g, g.dbg["dbg_YT"], g.YT, NCC, "yt")
        _dbg_dump2(g, g.dbg["dbg_cb"], g.cbrow[:, :], "cbrow")
def _p4(g, R):
    nc = g.nc
    wpq = R("wpq", [128, NCC, C], BF16)
    wpf = R("wpf", [128, 2, C], F32)
    wpy = R("wpy", [128, C], F32)
    wpp = R("wpp", [128, C], F32)
    wpn = R("wpn", [128, C], F32)
    wp_r = g.wprojT_d.rearrange("(n p) o -> n p o", p=128)

    with nc.Block("p4") as blk:
        _wfin(g, blk, g.sabs[:, 8:16], float(C) * C, 1e-8, 2, "wp")

        @blk.vector
        def _(v):
            g.sQ.wait(v)
            g.sV.inc(v.tensor_tensor(g.srow[0:1, 17:18], g.srow[0:1, 0:1],
                                     g.srow[0:1, 2:3], Alu.mult))
        _scalar_bcast(g, blk, 17, 10, "sap")

        d_wp = [0] * NCC
        v_wp = [0] * NCC
        for cc in range(NCC):
            @blk.scalar
            def _(s, cc=cc):
                if cc >= 2:
                    g.sV.wait_at(s, v_wp[cc - 2])
                d_wp[cc] = g.dwpf.inc(s.dma_start(out=wpf[:, cc % 2, :],
                                                  in_=wp_r[cc]), cc % 2)

            @blk.vector
            def _(v, cc=cc):
                g.dwpf.wait(v, d_wp[cc])
                v.tensor_scalar(wpy[:, :], wpf[:, cc % 2, :], g.bc[:, 3:4],
                                None, Alu.mult)
                v.tensor_scalar(wpp[:, :], wpy[:, :], 0.5, None, Alu.is_ge)
                v.scalar_tensor_tensor(wpp[:, :], wpy[:, :], 1.5, wpp[:, :],
                                       Alu.is_ge, Alu.add)
                v.tensor_scalar(wpn[:, :], wpy[:, :], -0.5, None, Alu.is_le)
                v.scalar_tensor_tensor(wpn[:, :], wpy[:, :], -1.5, wpn[:, :],
                                       Alu.is_le, Alu.add)
                g.sV.inc(v.tensor_tensor(wpq[:, cc, :], wpp[:, :], wpn[:, :],
                                         Alu.subtract))
                v_wp[cc] = g.sV.n

        with nc.psum_tensor("pP0", [128, T], F32) as pp0, \
             nc.psum_tensor("pP1", [128, T], F32) as pp1:
            pp = [pp0, pp1]
            e_p = [0] * NCC
            for oc in range(NCC):
                @blk.tensor
                def _(t, oc=oc):
                    g.sV.wait_at(t, v_wp[NCC - 1])
                    if oc >= 2:
                        g.sE.wait_at(t, e_p[oc - 2])
                    for cc in range(NCC):
                        lhsT = wpq[:, cc, 128 * oc:128 * oc + 128]
                        for th in range(4):
                            ins = t.matmul(pp[oc % 2][:, 512 * th:512 * th + 512],
                                     lhsT, g.YT[:, cc, 512 * th:512 * th + 512],
                                     start=(cc == 0), stop=(cc == NCC - 1))
                    g.sT.inc(ins)
                tm = g.sT.n

                @blk.vector
                def _(v, oc=oc, tm=tm):
                    g.sT.wait_at(v, tm)
                    g.sE.inc(v.tensor_scalar(g.attnT[:, oc, :], pp[oc % 2][:, :],
                                    g.bc[:, 10:11], None, Alu.mult))
                    e_p[oc] = g.sE.n

    if g.debug:
        _dbg_chunks(g, g.dbg["dbg_attnT"], g.attnT, NCC, "attnT")


def _p5(g, R):
    nc = g.nc
    g.d_h3o_last = [0, 0]
    A = R("Abuf", [128, 2, T + 2], F32)
    B = R("Bbuf", [128, 2, T], F32)
    Cb2 = R("Cbuf", [128, 2, T], F32)
    Mrun = R("MrunB", [128, T], F32)
    w1f = R("w1f", [128, 2, C], F32)
    w1p = R("w1p", [128, C], F32)
    w1n = R("w1n", [128, C], F32)
    w1q = R("w1q", [128, 2, C], BF16)
    w1_r = g.w1cm_d

    with nc.Block("p5pre") as blk:
        _wfin(g, blk, g.sabs[:, 80:80 + 2 * NJ], float(HID) * C, 1e-5, 4, "w1")

        @blk.vector
        def _(v):
            g.sQ.wait(v)
            ch = chained(g, v)
            ch.tensor_scalar(g.srow[0:1, 18:19], g.srow[0:1, 4:5],
                             1.0 / 127.0, None, Alu.mult)
            ch.tensor_scalar(g.srow[0:1, 24:25], g.srow[0:1, 4:5], 0.5, None,
                             Alu.mult)
            g.sV.inc(v.tensor_scalar(g.srow[0:1, 25:26], g.srow[0:1, 4:5],
                                     -0.5, None, Alu.mult))
        _scalar_bcast(g, blk, 18, 11, "w1f127")

        vth = g.sV.n
        with nc.psum_tensor("pth5", [128, 512], F32) as pth:
            @blk.tensor
            def _(t, vth=vth):
                g.sV.wait_at(t, vth)
                g.sT.inc(t.matmul(pth[:, 0:2], g.onesrow[:, :],
                                  g.srow[0:1, 24:26], start=True, stop=True))
            tth = g.sT.n

            @blk.vector
            def _(v, tth=tth):
                g.sT.wait_at(v, tth)
                g.sV.inc(v.tensor_copy(g.bc[:, 16:18], pth[:, 0:2]))

        @blk.vector
        def _(v):
            VC(g, v, v.tensor_scalar(g.sm[:, 8:8 + NT], g.clipcol[:, :],
                                     g.bc[:, 11:12], None, Alu.mult))
            v.memset(Mrun[:, :], 0.0)
            v.memset(A[:, :, 0:1], 0.0)
            g.sV.inc(v.memset(A[:, :, T + 1:T + 2], 0.0))
        vm0 = g.sV.n

        with nc.psum_tensor("ptc", [1, 512], F32) as ptc, \
             nc.psum_tensor("pic", [128, 512], F32) as pic:
            last_v = vm0
            for i in range(NT):
                @blk.tensor
                def _(t, i=i, last_v=last_v):
                    g.sV.wait_at(t, last_v)
                    g.sT.inc(t.matmul(ptc[0:1, 0:128], g.sm[:, 8 + i:9 + i],
                                      g.identf[:, :], start=True, stop=True))
                tm = g.sT.n

                @blk.vector
                def _(v, tm=tm):
                    g.sT.wait_at(v, tm)
                    g.sV.inc(v.tensor_copy(g.row[0:1, 0:128], ptc[0:1, 0:128]))
                vm2 = g.sV.n

                @blk.tensor
                def _(t, vm2=vm2):
                    g.sV.wait_at(t, vm2)
                    g.sT.inc(t.matmul(pic[:, 0:128], g.onesrow[:, :], g.row[0:1, 0:128],
                             start=True, stop=True))
                tm2 = g.sT.n

                @blk.vector
                def _(v, i=i, tm2=tm2):
                    g.sT.wait_at(v, tm2)
                    g.sV.inc(v.tensor_copy(g.inv1b[:, 128 * i:128 * i + 128], pic[:, 0:128]))
                last_v = g.sV.n

    with nc.Block("p5") as blk, \
         nc.psum_tensor("pH0", [128, T], F32) as ph0, \
         nc.psum_tensor("pH1", [128, T], F32) as ph1:
        ph = [ph0, ph1]
        d_w1 = [0] * NJ
        g_w1 = [0] * NJ
        v_w1 = [0] * NJ
        t_h1 = [0] * NJ
        e_h1 = [0] * NJ
        v_conv = [0] * NJ
        v_mr = [0] * NJ
        s_silu = [0] * NJ
        d_h3 = [0] * NJ

        @blk.scalar
        def _(s):
            for j in range(2):
                d_w1[j] = g.dw15.inc(s.dma_start(
                    out=w1f[:, j, :].rearrange("p (a jj) -> p a jj", a=NCC),
                    in_=w1_r[j].rearrange("(a p) jj -> p a jj", p=128)), j)

        def emit_quant(j):
            @blk.vector
            def _(v, j=j):
                g.dw15.wait(v, d_w1[j])
                if j >= 2:
                    g.sT.wait_at(v, t_h1[j - 2])
                v.tensor_scalar(w1p[:, :], w1f[:, j % 2, :], g.bc[:, 16:17],
                                None, Alu.is_ge)
                v.tensor_scalar(w1n[:, :], w1f[:, j % 2, :], g.bc[:, 17:18],
                                None, Alu.is_le)
                g.sV.inc(v.tensor_tensor(w1q[:, j % 2, :], w1p[:, :],
                                         w1n[:, :], Alu.subtract))
                v_w1[j] = g.sV.n

            @blk.scalar
            def _(s, j=j):
                if j + 2 < NJ:
                    g.sV.wait_at(s, v_w1[j])
                    d_w1[j + 2] = g.dw15.inc(s.dma_start(
                        out=w1f[:, j % 2, :].rearrange("p (a jj) -> p a jj", a=NCC),
                        in_=w1_r[j + 2].rearrange("(a p) jj -> p a jj", p=128)),
                        j % 2)

        emit_quant(0)
        emit_quant(1)
        for j in range(NJ):
            @blk.tensor
            def _(t, j=j):
                g.sV.wait_at(t, v_w1[j])
                if j >= 2:
                    g.sE.wait_at(t, e_h1[j - 2])
                for cc in range(NCC):
                    lhsT = w1q[:, j % 2, 128 * cc:128 * cc + 128]
                    for th in range(4):
                        ins = t.matmul(ph[j % 2][:, 512 * th:512 * th + 512],
                                 lhsT, g.aT[:, cc, 512 * th:512 * th + 512],
                                 start=(cc == 0), stop=(cc == NCC - 1))
                g.sT.inc(ins)
                t_h1[j] = g.sT.n

            @blk.vector
            def _(v, j=j):
                g.sT.wait_at(v, t_h1[j])
                g.sE.inc(v.tensor_tensor(A[:, j % 2, 1:T + 1], ph[j % 2][:, :],
                                         g.inv1b[:, :], Alu.mult))
                e_h1[j] = g.sE.n
                if j >= 2:
                    g.sS.wait_at(v, s_silu[j - 2])
                    g.sV.wait_at(v, v_mr[j - 2])
                v.tensor_scalar(B[:, j % 2, :], A[:, j % 2, 1:T + 1],
                                g.cwb[:, j, 1:2], g.cbb[:, j:j + 1],
                                Alu.mult, Alu.add)
                v.scalar_tensor_tensor(B[:, j % 2, :], A[:, j % 2, 0:T],
                                       g.cwb[:, j, 0:1], B[:, j % 2, :],
                                       Alu.mult, Alu.add)
                g.sV.inc(v.scalar_tensor_tensor(B[:, j % 2, :], A[:, j % 2, 2:T + 2],
                                                g.cwb[:, j, 2:3], B[:, j % 2, :],
                                                Alu.mult, Alu.add))
                v_conv[j] = g.sV.n

            @blk.scalar
            def _(s, j=j):
                g.sV.wait_at(s, v_conv[j])
                if j >= 2:
                    g.dh3o.wait(s, d_h3[j - 2])
                s.activation(Cb2[:, j % 2, :], B[:, j % 2, :], AF.Silu)
                g.sS.inc(s.activation(B[:, j % 2, :], Cb2[:, j % 2, :], AF.Abs))
                s_silu[j] = g.sS.n

            if j + 2 < NJ:
                emit_quant(j + 2)

            @blk.vector
            def _(v, j=j):
                g.sS.wait_at(v, s_silu[j])
                g.sV.inc(v.tensor_tensor(Mrun[:, :], Mrun[:, :], B[:, j % 2, :],
                                         Alu.max))
                v_mr[j] = g.sV.n

            @blk.scalar
            def _(s, j=j):
                g.sS.wait_at(s, s_silu[j])
                d_h3[j] = g.dh3o.inc(s.dma_start(out=g.h3scr[j],
                                                 in_=Cb2[:, j % 2, :]), j % 2)
                g.d_h3o_last[j % 2] = d_h3[j]

    with nc.Block("p6pre") as blk:
        @blk.vector
        def _(v):
            g.sV.inc(v.tensor_scalar(g.srow[0:1, 19:20], g.srow[0:1, 6:7], 1.0 / 127.0,
                            None, Alu.mult))
        _scalar_bcast(g, blk, 19, 12, "w2f127")

    with nc.Block("p6") as blk, \
         nc.psum_tensor("pm6", [128, 2048], F32) as pm6, \
         nc.psum_tensor("prR", [1, 2048], F32) as prR:
        vb12 = 0
        @blk.tensor
        def _(t):
            for i in range(NT):
                ins = t.matmul(pm6[:, 128 * i:128 * i + 128],
                               Mrun[:, 128 * i:128 * i + 128], g.identf[:, :],
                               start=(i % 4 == 0), stop=(i % 4 == 3))
            g.sT.inc(ins)
        t6a = g.sT.n

        @blk.vector
        def _(v, t6a=t6a, vb12=vb12):
            g.sT.wait_at(v, t6a)
            g.sV.wait_at(v, vb12)
            ch = chained(g, v)
            ch.reduce_max(g.sm[:, 0:16],
                          pm6[:, :].rearrange("p (a c) -> p a c", c=128), AX.X)
            ch.tensor_scalar(g.sm[:, 0:16], g.sm[:, 0:16], 1e-5, None, Alu.max)
            ch.reciprocal(g.sm[:, 16:32], g.sm[:, 0:16])
            ch.tensor_scalar(g.sm[:, 16:32], g.sm[:, 16:32], 127.0, None,
                             Alu.mult)
            g.sV.inc(v.tensor_scalar(g.sm[:, 0:16], g.sm[:, 0:16],
                                     g.bc[:, 12:13], None, Alu.mult))
        v6a = g.sV.n

        rowb = B[0:1, 0, :]
        vprev = v6a
        for rnd, (colr, dst) in enumerate([(g.sm[:, 16:32], g.s2b),
                                           (g.sm[:, 0:16], g.inv2b)]):
            @blk.tensor
            def _(t, vprev=vprev, colr=colr):
                g.sV.wait_at(t, vprev)
                for r in range(NT):
                    ins = t.matmul(prR[0:1, 128 * r:128 * r + 128],
                                   colr[:, r:r + 1], g.identf[:, :],
                                   start=(r % 4 == 0), stop=(r % 4 == 3))
                g.sT.inc(ins)
            t6b = g.sT.n

            @blk.vector
            def _(v, t6b=t6b):
                g.sT.wait_at(v, t6b)
                g.sV.inc(v.tensor_copy(rowb, prR[0:1, :]))
            v6b = g.sV.n

            @blk.tensor
            def _(t, v6b=v6b):
                g.sV.wait_at(t, v6b)
                for r in range(NT):
                    ins = t.matmul(pm6[:, 128 * r:128 * r + 128],
                                   g.onesrow[0:1, :],
                                   rowb[0:1, 128 * r:128 * r + 128],
                                   start=(r % 4 == 0), stop=(r % 4 == 3))
                g.sT.inc(ins)
            t6c = g.sT.n

            @blk.vector
            def _(v, t6c=t6c, dst=dst):
                g.sT.wait_at(v, t6c)
                ins = None
                for b in range(4):
                    ins = v.tensor_copy(dst[:, 512 * b:512 * b + 512],
                                        pm6[:, 512 * b:512 * b + 512])
                g.sV.inc(ins)
            vprev = g.sV.n

    if g.debug:
        _dbg_dump2(g, g.dbg["dbg_s2b"], g.s2b[:, :], "s2b")
def _p7(g, R):
    nc = g.nc
    W2q = R("W2q", [128, NJ, C], BF16)
    h3b = R("h3b", [128, 4, 512], F32)
    bvF = R("bvF", [128, 8, 512], F32)
    bqi = R("bqi", [128, 512], I32)
    bcf = R("bcf", [128, 4, 512], F32)
    bp_ = R("bp_", [128, 512], F32)
    bn_ = R("bn_", [128, 512], F32)
    bq = R("bq", [128, 2, 512], BF16)
    NB = 4

    with nc.Block("p7") as blk, \
         nc.psum_tensor("pO", [128, 8 * 512], F32) as pO:
        @blk.sync
        def _(sy):
            g.dw2qo.final_wait(sy)
            g.dw2q7.inc(sy.dma_start(out=W2q[:, :, :],
                                     in_=g.w2qscr.rearrange("n p o -> p n o")), 0)
        d_w2q = (0, g.dw2q7.sems[0].n)

        e_out = 0
        prev_v_bq = [0] * NJ
        prev_t_mm = [0] * NJ
        prev_s_cf = [0] * NJ
        for tt_ in range(4):
            d_h3 = [0] * NJ
            v_bv = [0] * NJ
            s_cf = [0] * NJ
            v_bq = [0] * NJ
            t_mm = [0] * NJ

            @blk.sync
            def _(sy, tt_=tt_):
                if tt_ == 0:
                    g.dh3o.final_wait(sy)
                for j in range(NB):
                    if tt_ > 0:
                        g.sV.wait_at(sy, prev_v_bq[NJ - NB + j])
                    d_h3[j] = g.dh3i.inc(sy.dma_start(
                        out=h3b[:, j, :],
                        in_=g.h3scr[j][:, 512 * tt_:512 * tt_ + 512]), j % 4)

            def emit_bv(b):
                @blk.vector
                def _(v, b=b, tt_=tt_):
                    for j in range(NB * b, NB * b + NB):
                        g.dh3i.wait(v, d_h3[j])
                        if j >= 8:
                            g.sV.wait_at(v, v_bq[j - 8])
                        elif tt_ > 0:
                            g.sV.wait_at(v, prev_v_bq[NJ - 8 + j])
                        g.sV.inc(v.tensor_tensor(
                            bvF[:, j % 8, :], h3b[:, j % NB, :],
                            g.s2b[:, 512 * tt_:512 * tt_ + 512], Alu.mult))
                        v_bv[j] = g.sV.n

                @blk.sync
                def _(sy, b=b, tt_=tt_):
                    for j in range(NB * b, NB * b + NB):
                        if j + NB < NJ:
                            g.sV.wait_at(sy, v_bv[j])
                            d_h3[j + NB] = g.dh3i.inc(sy.dma_start(
                                out=h3b[:, j % NB, :],
                                in_=g.h3scr[j + NB][:, 512 * tt_:512 * tt_ + 512]),
                                j % 4)

            emit_bv(0)
            for b in range(NJ // NB):
                @blk.scalar
                def _(s, b=b, tt_=tt_):
                    for j in range(NB * b, NB * b + NB):
                        g.sV.wait_at(s, v_bv[j])
                        if b > 0:
                            g.sV.wait_at(s, v_bq[j - NB])
                        elif tt_ > 0:
                            g.sV.wait_at(s, prev_v_bq[NJ - NB + j])
                        s.activation(bqi[:, :], bvF[:, j % 8, :],
                                     AF.Copy)
                        g.sS.inc(s.activation(bcf[:, j % NB, :],
                                              bqi[:, :], AF.Copy))
                        s_cf[j] = g.sS.n

                if b + 1 < NJ // NB:
                    emit_bv(b + 1)

                for j in range(NB * b, NB * b + NB):
                    @blk.vector
                    def _(v, j=j, tt_=tt_):
                        g.sS.wait_at(v, s_cf[j])
                        if j >= 2:
                            g.sT.wait_at(v, t_mm[j - 2])
                        elif tt_ > 0:
                            g.sT.wait_at(v, prev_t_mm[NJ - 2 + j])
                        v.scalar_tensor_tensor(bp_[:, :], bvF[:, j % 8, :],
                                               -0.5, bcf[:, j % NB, :],
                                               Alu.add, Alu.is_ge)
                        v.scalar_tensor_tensor(bn_[:, :], bvF[:, j % 8, :],
                                               0.5, bcf[:, j % NB, :],
                                               Alu.add, Alu.is_le)
                        v.tensor_tensor(bcf[:, j % NB, :], bcf[:, j % NB, :],
                                        bp_[:, :], Alu.add)
                        g.sV.inc(v.tensor_tensor(bq[:, j % 2, :],
                                                 bcf[:, j % NB, :],
                                                 bn_[:, :], Alu.subtract))
                        v_bq[j] = g.sV.n

                    @blk.tensor
                    def _(t, j=j):
                        g.sV.wait_at(t, v_bq[j])
                        if j == 0:
                            g.dw2q7.wait(t, d_w2q)
                            g.sE.wait_at(t, e_out)
                        for oc in range(NCC):
                            ins = t.matmul(pO[:, 512 * oc:512 * oc + 512],
                                     W2q[:, j, 128 * oc:128 * oc + 128],
                                     bq[:, j % 2, :],
                                     start=(j == 0), stop=(j == NJ - 1))
                        g.sT.inc(ins)
                        t_mm[j] = g.sT.n

            @blk.vector
            def _(v, tt_=tt_, t_mm=t_mm):
                g.sT.wait_at(v, t_mm[NJ - 1])
                for oc in range(NCC):
                    v.tensor_tensor(bp_[:, :], pO[:, 512 * oc:512 * oc + 512],
                                    g.inv2b[:, 512 * tt_:512 * tt_ + 512], Alu.mult)
                    ins = v.tensor_tensor(g.attnT[:, oc, 512 * tt_:512 * tt_ + 512],
                                    bp_[:, :],
                                    g.attnT[:, oc, 512 * tt_:512 * tt_ + 512],
                                    Alu.add)
                g.sE.inc(ins)
            e_out = g.sE.n
            prev_v_bq = v_bq
            prev_t_mm = t_mm
            prev_s_cf = s_cf


def _p8(g, R):
    nc = g.nc
    xr = R("xr8", [128, 4, C], F32)
    ob = R("ob8", [128, 8, C], F32)
    x_r = g.x_d.rearrange("(n p) c -> n p c", p=128)
    out_r = g.out_d.rearrange("(n p) c -> n p c", p=128)

    with nc.Block("p8") as blk, \
         nc.psum_tensor("pX0", [128, 1024], F32) as px0, \
         nc.psum_tensor("pX1", [128, 1024], F32) as px1:
        px = [px0, px1]
        d_x = [0] * 4
        v_add = [0] * NT
        d_out = [0] * 4

        @blk.sync
        def _(sy):
            d_x[0] = g.dx8.inc(sy.dma_start(
                out=xr[:, :, :], in_=x_r[0:4].rearrange("n p c -> p n c")), 0)

        for b4 in range(4):
            for ti in range(4):
                i = 4 * b4 + ti

                @blk.tensor
                def _(t, i=i):
                    if i >= 2:
                        g.sV.wait_at(t, v_add[i - 2])
                    for oc in range(NCC):
                        ins = t.matmul(px[i % 2][:, 128 * oc:128 * oc + 128],
                                       g.attnT[:, oc, 128 * i:128 * i + 128],
                                       g.identf[:, :],
                                       start=(oc % 4 == 0), stop=(oc % 4 == 3))
                    g.sT.inc(ins)
                tm = g.sT.n

                @blk.vector
                def _(v, i=i, ti=ti, b4=b4, tm=tm):
                    g.sT.wait_at(v, tm)
                    g.dx8.wait(v, d_x[b4])
                    if b4 >= 2:
                        g.dout.wait(v, d_out[b4 - 2])
                    g.sV.inc(v.tensor_tensor(ob[:, 4 * (b4 % 2) + ti, :], px[i % 2][:, :],
                                    xr[:, ti, :], Alu.add))
                    v_add[i] = g.sV.n

            @blk.sync
            def _(sy, b4=b4):
                if b4 + 1 < 4:
                    g.sV.wait_at(sy, v_add[4 * b4 + 3])
                    d_x[b4 + 1] = g.dx8.inc(sy.dma_start(
                        out=xr[:, :, :],
                        in_=x_r[4 * b4 + 4:4 * b4 + 8].rearrange("n p c -> p n c")),
                        0)

            @blk.scalar
            def _(s, b4=b4):
                g.sV.wait_at(s, v_add[4 * b4 + 3])
                d_out[b4] = g.dout.inc(s.dma_start(
                    out=out_r[4 * b4:4 * b4 + 4].rearrange("n p c -> p n c"),
                    in_=ob[:, 4 * (b4 % 2):4 * (b4 % 2) + 4, :]), b4 % 2)

        @blk.sync
        def _(sy):
            for ds in g.dstreams.values():
                ds.final_wait(sy)
            g.sD3.wait(sy)


def kernel(L0_attn_x, L0_attn_w_attn, L0_attn_w_proj, L0_ffn_w1, L0_ffn_conv_w,
           L0_ffn_conv_b, L0_ffn_w2):
    x = np.ascontiguousarray(np.asarray(L0_attn_x, dtype=np.float32))
    wattnT = np.ascontiguousarray(np.asarray(L0_attn_w_attn, np.float32).T)
    wprojT = np.ascontiguousarray(np.asarray(L0_attn_w_proj, np.float32).T)
    w1 = np.asarray(L0_ffn_w1, np.float32)
    w1cm = np.ascontiguousarray(w1.T.reshape(C, NJ, 128).transpose(1, 0, 2))
    w2T = np.ascontiguousarray(np.asarray(L0_ffn_w2, np.float32).T)
    convw = np.ascontiguousarray(
        np.asarray(L0_ffn_conv_w, np.float32).reshape(NJ, 128, 3))
    convb = np.ascontiguousarray(
        np.asarray(L0_ffn_conv_b, np.float32).reshape(NJ, 128))

    nc = build_nc()
    in_maps = [{"x": np.ascontiguousarray(x[b]), "wattnT": wattnT,
                "wprojT": wprojT, "w1cm": w1cm, "w2T": w2T,
                "convw": convw, "convb": convb} for b in range(8)]
    res = run_bass_kernel_spmd(nc, in_maps, core_ids=list(range(8)))
    return np.stack([res.results[b]["out"] for b in range(8)], axis=0)


# version 16
# speedup vs baseline: 1.0220x; 1.0220x over previous
import contextlib
import os
import numpy as np

import concourse.bass as bass
import concourse.mybir as mybir
from concourse.bass_utils import run_bass_kernel_spmd

F32 = mybir.dt.float32
BF16 = mybir.dt.bfloat16
I32 = mybir.dt.int32
Alu = mybir.AluOpType
AF = mybir.ActivationFunctionType
AX = mybir.AxisListType

PARANOID = bool(int(os.environ.get("KPARANOID", "0")))
T = 2048
C = 1024
H = 16
D = 64
HID = 4096
NT = T // 128
NCC = C // 128
NJ = HID // 128
NATT = 3 * C
NEG = -1.0e38
EPS_LN = 1e-5


class SemT:

    def __init__(self, nc, es, name):
        self.h = es.enter_context(nc.semaphore(name))
        self.n = 0

    def inc(self, instr, k=1):
        instr.then_inc(self.h, k)
        self.n += k
        return self.n

    def wait(self, eng):
        eng.wait_ge(self.h, self.n)

    def wait_at(self, eng, val):
        if val > 0:
            eng.wait_ge(self.h, val)


class G:
    pass


def VC(g, eng, instr):
    g.sQ.inc(instr)
    g.sQ.wait(eng)
    return instr


def chained(g, eng):
    class _P:
        def __getattr__(self, name):
            f = getattr(eng, name)

            def call(*a, **k):
                return VC(g, eng, f(*a, **k))
            return call
    return _P()


class DSlots:

    def __init__(self, g, name, n):
        self.sems = [SemT(g.nc, g.es, "%s%d" % (name, i)) for i in range(n)]

    def inc(self, instr, slot):
        return (slot, self.sems[slot].inc(instr, 16))

    def wait(self, eng, mark):
        if mark == 0 or mark is None:
            return
        slot, val = mark
        self.sems[slot].wait_at(eng, val)

    def final_wait(self, eng):
        for s in self.sems:
            s.wait(eng)


def build_nc(debug=False, stop_after=None):
    nc = bass.Bass("TRN2", target_bir_lowering=False,
                   detect_race_conditions=False)
    g = G()
    g.nc = nc
    g.debug = debug
    g.stop_after = stop_after

    g.x_d = nc.dram_tensor("x", [T, C], F32, kind="ExternalInput").ap()
    g.wattnT_d = nc.dram_tensor("wattnT", [C, NATT], F32, kind="ExternalInput").ap()
    g.wprojT_d = nc.dram_tensor("wprojT", [C, C], F32, kind="ExternalInput").ap()
    g.w1cm_d = nc.dram_tensor("w1cm", [NJ, C, 128], F32, kind="ExternalInput").ap()
    g.w2T_d = nc.dram_tensor("w2T", [HID, C], F32, kind="ExternalInput").ap()
    g.convw_d = nc.dram_tensor("convw", [NJ, 128, 3], F32, kind="ExternalInput").ap()
    g.convb_d = nc.dram_tensor("convb", [NJ, 128], F32, kind="ExternalInput").ap()
    g.out_d = nc.dram_tensor("out", [T, C], F32, kind="ExternalOutput").ap()

    g.h3scr = nc.dram_tensor("h3scr", [NJ, 128, T], F32, kind="Internal").ap()
    g.wqascr = nc.dram_tensor("wqascr", [NCC, 128, NATT], BF16, kind="Internal").ap()
    g.w2qscr = nc.dram_tensor("w2qscr", [NJ, 128, C], BF16, kind="Internal").ap()

    g.dbg = {}
    if debug:
        for name, shape, dt_ in [
            ("dbg_xnT", [NCC * 128, T], BF16), ("dbg_aT", [NCC * 128, T], BF16),
            ("dbg_QT", [NCC * 128, T], BF16), ("dbg_KT", [NCC * 128, T], BF16),
            ("dbg_Vb", [NT * 128, H * 66], BF16), ("dbg_scales", [128, 16], F32),
            ("dbg_YT", [NCC * 128, T], BF16), ("dbg_attnT", [NCC * 128, T], F32),
            ("dbg_s2b", [128, T], F32), ("dbg_cb", [128, 64], F32),
        ]:
            g.dbg[name] = nc.dram_tensor(name, shape, dt_, kind="ExternalOutput").ap()

    with contextlib.ExitStack() as es:
        g.es = es
        for s in ["sD", "sD2", "sD3", "sV", "sS", "sT", "sG", "sE", "sP", "sQ"]:
            setattr(g, s, SemT(nc, es, s))
        g.dstreams = {}
        for name, n in [("dx", 3), ("dwa", 2), ("dwqo", 2), ("dwp", 2),
                        ("dmisc", 2), ("dwqa", 1), ("dw13", 2), ("dw23", 4),
                        ("dw2qo", 2), ("dwpf", 2), ("dw15", 2), ("dh3o", 2),
                        ("dw2q7", 1), ("dh3i", 4), ("dx8", 1), ("dout", 2)]:
            g.dstreams[name] = DSlots(g, name, n)
            setattr(g, name, g.dstreams[name])

        Lf = lambda *a, **k: es.enter_context(nc.sbuf_tensor(*a, side="left", **k))
        g.Lf = Lf
        g.identf = Lf("identf", [128, 128], F32)
        g.identb = Lf("identb", [128, 128], BF16)
        g.trinb = Lf("trinb", [128, 128], BF16)
        g.onescol = Lf("onescol", [128, 1], F32)
        g.onesrow = Lf("onesrow", [1, 128], F32)
        g.pcol = Lf("pcol", [128, 1], F32)
        g.cwb = Lf("cwb", [128, NJ, 3], F32)
        g.cbb = Lf("cbb", [128, NJ], F32)
        g.clipcol = Lf("clipcol", [128, NT], F32)
        g.s1col = Lf("s1col", [128, NT], F32)
        g.sm = Lf("sm", [128, 32], F32)
        g.row = Lf("rowscr", [1, 256], F32)
        g.srow = Lf("srow", [1, 32], F32)
        g.bc = Lf("bc", [128, 24], F32)
        g.sabs = Lf("sabs", [128, 144], F32)
        g.cbrow = Lf("cbrow", [128, 64], F32)

        _const_phase(g)

        def mkR(stack):
            return lambda *a, **k: stack.enter_context(
                nc.sbuf_tensor(*a, side="right", **k))

        SEQ = ["p0", "p1", "p2", "p3", "p4", "p5", "p7", "p8"]
        lim = SEQ.index(stop_after) + 1 if stop_after else len(SEQ)

        def on(ph):
            return SEQ.index(ph) < lim

        aTs = contextlib.ExitStack()
        g.aT = aTs.enter_context(
            nc.sbuf_tensor("aT", [128, NCC, T], BF16, side="right"))
        with contextlib.ExitStack() as ps03:
            with contextlib.ExitStack() as qkvs:
                Lq = lambda *a, **k: qkvs.enter_context(
                    nc.sbuf_tensor(*a, side="left", **k))
                with contextlib.ExitStack() as p01r:
                    g.xnT = p01r.enter_context(
                        nc.sbuf_tensor("xnT", [128, NCC, T], BF16, side="right"))
                    if on("p0"):
                        with contextlib.ExitStack() as s_:
                            _p0(g, mkR(s_))
                    if on("p1"):
                        g.QT = Lq("QT", [128, NCC, T], BF16)
                        g.KT = Lq("KT", [128, NCC, T], BF16)
                        g.Vb = Lq("Vb", [128, NT, H * 66], BF16)
                        with contextlib.ExitStack() as s_:
                            _p1(g, mkR(s_))
                    if on("p2"):
                        with contextlib.ExitStack() as s_:
                            _p2(g, mkR(s_))
                if on("p3"):
                    g.PT = Lq("PT", [128, 28, 512], BF16)
                    g.YT = ps03.enter_context(
                        nc.sbuf_tensor("YT", [128, NCC, T], BF16, side="right"))
                    with contextlib.ExitStack() as s_:
                        _p3(g, mkR(s_))
            if on("p4"):
                g.attnT = Lf("attnT", [128, NCC, T], F32)
                with contextlib.ExitStack() as s_:
                    _p4(g, mkR(s_))
        if on("p5"):
            g.s2b = Lf("s2b", [128, T], F32)
            g.inv2b = Lf("inv2b", [128, T], F32)
            g.inv1b = Lf("inv1b", [128, T], F32)
            with contextlib.ExitStack() as s_:
                _p5(g, mkR(s_))
        aTs.close()
        if on("p7"):
            with contextlib.ExitStack() as s_:
                _p7(g, mkR(s_))
        if on("p8"):
            with contextlib.ExitStack() as s_:
                _p8(g, mkR(s_))

    return nc


def _const_phase(g):
    nc = g.nc
    with nc.Block("const") as blk:
        @blk.gpsimd
        def _(gp):
            g.sG.inc(gp.iota(g.identf[:, :], [[1, 128]], channel_multiplier=0,
                             allow_small_or_imprecise_dtypes=True), 0) if False else \
                gp.iota(g.identf[:, :], [[1, 128]], channel_multiplier=0,
                        allow_small_or_imprecise_dtypes=True)
            g.sG.inc(gp.iota(g.pcol[:, :], [[0, 1]], channel_multiplier=1,
                             allow_small_or_imprecise_dtypes=True), 2)

        @blk.vector
        def _(v):
            g.sG.wait(v)
            v.tensor_scalar(g.identb[:, :], g.identf[:, :], g.pcol[:, :], None,
                            Alu.is_equal)
            v.tensor_scalar(g.trinb[:, :], g.identf[:, :], g.pcol[:, :], NEG,
                            Alu.is_lt, Alu.mult)
            v.tensor_scalar(g.identf[:, :], g.identf[:, :], g.pcol[:, :], None,
                            Alu.is_equal)
            v.memset(g.onescol[:, :], 1.0)
            v.memset(g.onesrow[:, :], 1.0)
            v.memset(g.sabs[:, :], 0.0)
            v.memset(g.sm[:, :], 0.0)
            v.memset(g.row[:, :], 0.0)
            v.memset(g.srow[:, :], 0.0)
            v.memset(g.bc[:, :], 0.0)
            v.memset(g.cbrow[:, :], 0.0)
            v.memset(g.clipcol[:, :], 0.0)
            v.memset(g.s1col[:, :], 0.0)
def _wfin(g, blk, cols, nelem, clip_lo, slot, name):
    nc = g.nc

    @blk.vector
    def _(v):
        g.sV.inc(v.reduce_sum(g.sm[:, 0:1], cols, AX.X))
    vm = g.sV.n

    with nc.psum_tensor("pw1_" + name, [1, 512], F32) as p1, \
         nc.psum_tensor("pw2_" + name, [128, 512], F32) as pb:
        @blk.tensor
        def _(t):
            g.sV.wait_at(t, vm)
            g.sT.inc(t.matmul(p1[:, 0:1], g.onescol[:, :], g.sm[:, 0:1], start=True, stop=True))
        tm = g.sT.n

        @blk.vector
        def _(v):
            g.sT.wait_at(v, tm)
            ch = chained(g, v)
            ch.tensor_scalar(g.srow[0:1, slot:slot + 1], p1[:, 0:1], 1.0 / nelem,
                             clip_lo, Alu.mult, Alu.max)
            g.sV.inc(v.reciprocal(g.srow[0:1, slot + 1:slot + 2],
                                   g.srow[0:1, slot:slot + 1]))
        vm2 = g.sV.n

        @blk.tensor
        def _(t):
            g.sV.wait_at(t, vm2)
            g.sT.inc(t.matmul(pb[:, 0:2], g.onesrow[:, :], g.srow[0:1, slot:slot + 2],
                     start=True, stop=True))
        tm2 = g.sT.n

        @blk.vector
        def _(v):
            g.sT.wait_at(v, tm2)
            g.sV.inc(v.tensor_copy(g.bc[:, slot:slot + 2], pb[:, 0:2]))


def _scalar_bcast(g, blk, src_slot, dst_slot, name):
    nc = g.nc
    vm = g.sV.n
    with nc.psum_tensor("pb_" + name, [128, 512], F32) as pb:
        @blk.tensor
        def _(t):
            g.sV.wait_at(t, vm)
            g.sT.inc(t.matmul(pb[:, 0:1], g.onesrow[:, :], g.srow[0:1, src_slot:src_slot + 1],
                     start=True, stop=True))
        tm = g.sT.n

        @blk.vector
        def _(v):
            g.sT.wait_at(v, tm)
            g.sV.inc(v.tensor_copy(g.bc[:, dst_slot:dst_slot + 1], pb[:, 0:1]))


def _dbg_dump2(g, out_ap, in_ap, name):
    if not g.debug:
        return
    nc = g.nc
    with nc.Block("dbg_" + name) as blk:
        @blk.gpsimd
        def _(gp):
            g.sD3.inc(gp.dma_start(out=out_ap, in_=in_ap), 16)
            g.sD3.wait(gp)


def _dbg_chunks(g, dram, sbuf3, nchunks, name):
    if not g.debug:
        return
    nc = g.nc
    with nc.Block("dbgc_" + name) as blk:
        @blk.gpsimd
        def _(gp):
            for n in range(nchunks):
                g.sD3.inc(gp.dma_start(
                    out=dram.rearrange("(n p) w -> n p w", p=128)[n],
                    in_=sbuf3[:, n, :]), 16)
            g.sD3.wait(gp)


def _p0(g, R):
    g.d_wqo_last = [0, 0]
    nc = g.nc
    xring = R("xring", [128, 3, C], F32)
    xnb = R("xnb", [128, 2, C], BF16)
    ai = R("ai", [128, C], I32)
    ab = R("ab", [128, 2, C], BF16)
    avF = R("avF", [128, C], F32)
    acf = R("acf", [128, C], F32)
    ap_ = R("ap_", [128, C], F32)
    an_ = R("an_", [128, C], F32)
    wsb = R("wsb", [128, 2, NATT], F32)
    wpos = R("wpos", [128, NATT], F32)
    wneg = R("wneg", [128, NATT], F32)
    wqab = R("wqab", [128, 2, NATT], BF16)
    wstr = R("wstr", [128, 2, C], F32)
    sink3 = R("sink3", [128, NATT], F32)
    SMa = R("SMa", [128, 16], F32)
    SMb = R("SMb", [128, 16], F32)
    AMX = R("AMX", [128, 16], F32)
    rcol = R("rcol", [128, 16], F32)
    rt0 = R("rt0", [128, 16], F32)
    mrcol = R("mrcol", [128, 16], F32)
    tm16 = R("tm16", [128, 16], F32)
    rs = R("rs", [128, 16], F32)
    mrs = R("mrs", [128, 16], F32)

    x_r = g.x_d.rearrange("(n p) c -> n p c", p=128)
    wa_r = g.wattnT_d.rearrange("(n p) o -> n p o", p=128)
    wp_r = g.wprojT_d.rearrange("(n p) o -> n p o", p=128)
    w2_r = g.w2T_d.rearrange("(n p) o -> n p o", p=128)

    d_x = [0] * 48
    v_cons = [0] * 48
    d_wa = [0] * NCC
    s_wa = [0] * NCC
    d_w2a = [0] * NJ
    s_w2a = [0] * NJ
    t_x = [0] * NT
    t_a = [0] * NT
    s_xe = [0] * NT
    s_ae = [0] * NT

    with nc.Block("p0") as blk, \
         nc.psum_tensor("pxa", [128, 512], F32) as px0, \
         nc.psum_tensor("pxb", [128, 512], F32) as px1, \
         nc.psum_tensor("paa", [128, 512], F32) as pa0, \
         nc.psum_tensor("pab", [128, 512], F32) as pa1:
        pxs = [px0, px1]
        pas = [pa0, pa1]

        @blk.sync
        def _(sy):
            g.dmisc.inc(sy.dma_start(out=g.cwb[:, :, :],
                                     in_=g.convw_d.rearrange("n p k -> p n k"),
                                     allow_slow_non_contiguous=True), 0)
            g.dmisc.inc(sy.dma_start(out=g.cbb[:, :],
                                     in_=g.convb_d.rearrange("n p -> p n"),
                                     allow_slow_non_contiguous=True), 1)
            for L in range(3):
                d_x[L] = g.dx.inc(sy.dma_start(out=xring[:, L % 3, :],
                                               in_=x_r[L]), L % 3)

        @blk.scalar
        def _(sc):
            for cc in range(2):
                d_wa[cc] = g.dwa.inc(sc.dma_start(out=wsb[:, cc, :],
                                                  in_=wa_r[cc]), cc)

        def prefetch(L):
            @blk.sync
            def _(sy, L=L):
                if L + 3 < 48:
                    g.sV.wait_at(sy, v_cons[L])
                    d_x[L + 3] = g.dx.inc(sy.dma_start(
                        out=xring[:, (L + 3) % 3, :], in_=x_r[(L + 3) % 16]),
                        (L + 3) % 3)

        for i in range(NT):
            @blk.vector
            def _(v, i=i):
                g.dx.wait(v, d_x[i])
                v.reduce_sum(SMa[:, i:i + 1], xring[:, i % 3, :], AX.X)
                g.sV.inc(v.scalar_tensor_tensor(sink3[:, 0:C], xring[:, i % 3, :],
                                                1.0, xring[:, i % 3, :],
                                                Alu.mult, Alu.mult,
                                                accum_out=SMb[:, i:i + 1]))
                v_cons[i] = g.sV.n
            prefetch(i)

            if i % 2 == 1 and (i - 1) // 2 < NCC:
                cc = (i - 1) // 2

                @blk.scalar
                def _(s, cc=cc):
                    g.dwa.wait(s, d_wa[cc])
                    g.sS.inc(s.activation(sink3[:, :], wsb[:, cc % 2, :], AF.Abs,
                                          accum_out=g.sabs[:, cc:cc + 1]))
                    s_wa[cc] = g.sS.n
                    if cc + 2 < NCC:
                        d_wa[cc + 2] = g.dwa.inc(
                            s.dma_start(out=wsb[:, cc % 2, :], in_=wa_r[cc + 2]),
                            cc % 2)

        @blk.vector
        def _(v, w=v_cons[NT - 1]):
            g.sV.wait_at(v, w)
            ch = chained(g, v)
            ch.tensor_scalar(SMa[:, :], SMa[:, :], 1.0 / C, None, Alu.mult)
            ch.tensor_scalar(SMb[:, :], SMb[:, :], 1.0 / C, None, Alu.mult)
            ch.tensor_tensor(tm16[:, :], SMa[:, :], SMa[:, :], Alu.mult)
            ch.tensor_tensor(SMb[:, :], SMb[:, :], tm16[:, :], Alu.subtract)
            ch.tensor_scalar(SMb[:, :], SMb[:, :], EPS_LN, None, Alu.add)
            g.sV.inc(v.reciprocal(rcol[:, :], SMb[:, :]))
        v_b1 = g.sV.n

        @blk.scalar
        def _(s):
            g.sV.wait_at(s, v_b1)
            g.sS.inc(s.activation(rt0[:, :], rcol[:, :], AF.Sqrt))
        s_b = g.sS.n

        @blk.vector
        def _(v):
            g.sS.wait_at(v, s_b)
            ch = chained(g, v)
            ch.reciprocal(tm16[:, :], rt0[:, :])
            ch.tensor_tensor(tm16[:, :], rcol[:, :], tm16[:, :], Alu.mult)
            ch.tensor_tensor(tm16[:, :], rt0[:, :], tm16[:, :], Alu.add)
            ch.tensor_scalar(rcol[:, :], tm16[:, :], 0.5, None, Alu.mult)
            ch.tensor_tensor(mrcol[:, :], SMa[:, :], rcol[:, :], Alu.mult)
            g.sV.inc(v.tensor_scalar(mrcol[:, :], mrcol[:, :], -1.0, None,
                                     Alu.mult))
        v_b2 = g.sV.n

        @blk.scalar
        def _(s):
            for m in range(2):
                g.sS.wait_at(s, s_wa[NCC - 2 + m])
                d_w2a[m] = g.dw23.inc(s.dma_start(out=wsb[:, m, 0:C],
                                                  in_=w2_r[m]), m)

        def w2chunk(m):
            @blk.scalar
            def _(s, m=m):
                g.dw23.wait(s, d_w2a[m])
                g.sS.inc(s.activation(sink3[:, 0:C], wsb[:, m % 2, 0:C],
                                      AF.Abs,
                                      accum_out=g.sabs[:, 48 + m:49 + m]))
                s_w2a[m] = g.sS.n
                if m + 2 < NJ:
                    d_w2a[m + 2] = g.dw23.inc(s.dma_start(
                        out=wsb[:, m % 2, 0:C], in_=w2_r[m + 2]), m % 2)

        for i in range(NT):
            L = 16 + i

            @blk.vector
            def _(v, i=i, L=L):
                g.dx.wait(v, d_x[L])
                if i == 0:
                    g.sV.wait_at(v, v_b2)
                if i >= 2:
                    g.sT.wait_at(v, t_x[i - 2])
                v.tensor_scalar(xnb[:, i % 2, :], xring[:, L % 3, :],
                                rcol[:, i:i + 1], mrcol[:, i:i + 1],
                                Alu.mult, Alu.add)
                g.sV.inc(v.reduce_max(AMX[:, i:i + 1], xnb[:, i % 2, :], AX.X,
                                      apply_absolute_value=True))
                v_cons[L] = g.sV.n
            v_c = g.sV.n
            prefetch(L)

            @blk.tensor
            def _(t, i=i, v_c=v_c):
                g.sV.wait_at(t, v_c)
                if i >= 1:
                    g.sS.wait_at(t, s_xe[i - 1])
                for half in range(2):
                    for q in range(4):
                        cc = 4 * half + q
                        ins = t.matmul(pxs[half][:, 128 * q:128 * q + 128],
                                       xnb[:, i % 2, 128 * cc:128 * cc + 128],
                                       g.identb[:, :],
                                       start=(q == 0), stop=(q == 3))
                g.sT.inc(ins)
                t_x[i] = g.sT.n

            @blk.scalar
            def _(s, i=i):
                g.sT.wait_at(s, t_x[i])
                for half in range(2):
                    ins = s.activation(
                        g.xnT[:, 4 * half:4 * half + 4, 128 * i:128 * i + 128],
                        pxs[half][:, :].rearrange("p (n c) -> p n c", c=128),
                        AF.Copy)
                g.sS.inc(ins)
                s_xe[i] = g.sS.n
            w2chunk(i)

        @blk.vector
        def _(v, w=v_cons[31]):
            g.sV.wait_at(v, w)
            ch = chained(g, v)
            ch.tensor_scalar(g.clipcol[:, 0:NT], AMX[:, :], 1e-5, None, Alu.max)
            ch.reciprocal(tm16[:, :], g.clipcol[:, 0:NT])
            ch.tensor_scalar(g.s1col[:, 0:NT], tm16[:, :], 127.0, None,
                             Alu.mult)
            ch.tensor_tensor(rs[:, :], rcol[:, :], g.s1col[:, 0:NT], Alu.mult)
            ch.tensor_tensor(mrs[:, :], mrcol[:, :], g.s1col[:, 0:NT], Alu.mult)
            g.sV.inc(v.tensor_copy(tm16[:, :], rs[:, :]))
        v_d = g.sV.n

        for i in range(NT):
            L = 32 + i

            @blk.vector
            def _(v, i=i, L=L):
                g.dx.wait(v, d_x[L])
                if i == 0:
                    g.sV.wait_at(v, v_d)
                if i >= 2:
                    g.sT.wait_at(v, t_a[i - 2])
                v.tensor_scalar(avF[:, :], xring[:, L % 3, :],
                                rs[:, i:i + 1], mrs[:, i:i + 1],
                                Alu.mult, Alu.add)
                v.tensor_copy(ai[:, :], avF[:, :])
                v.tensor_copy(acf[:, :], ai[:, :])
                v.scalar_tensor_tensor(ap_[:, :], avF[:, :], -0.5, acf[:, :],
                                       Alu.add, Alu.is_ge)
                v.scalar_tensor_tensor(an_[:, :], avF[:, :], 0.5, acf[:, :],
                                       Alu.add, Alu.is_le)
                v.tensor_tensor(acf[:, :], acf[:, :], ap_[:, :], Alu.add)
                g.sV.inc(v.tensor_tensor(ab[:, i % 2, :], acf[:, :], an_[:, :],
                                         Alu.subtract))
                v_cons[L] = g.sV.n
            v_e = g.sV.n
            prefetch(L)

            @blk.tensor
            def _(t, i=i, v_e=v_e):
                g.sV.wait_at(t, v_e)
                if i >= 1:
                    g.sS.wait_at(t, s_ae[i - 1])
                for half in range(2):
                    for q in range(4):
                        cc = 4 * half + q
                        ins = t.matmul(pas[half][:, 128 * q:128 * q + 128],
                                       ab[:, i % 2, 128 * cc:128 * cc + 128],
                                       g.identb[:, :],
                                       start=(q == 0), stop=(q == 3))
                g.sT.inc(ins)
                t_a[i] = g.sT.n

            @blk.scalar
            def _(s, i=i):
                g.sT.wait_at(s, t_a[i])
                for half in range(2):
                    ins = s.activation(
                        g.aT[:, 4 * half:4 * half + 4, 128 * i:128 * i + 128],
                        pas[half][:, :].rearrange("p (n c) -> p n c", c=128),
                        AF.Copy)
                g.sS.inc(ins)
                s_ae[i] = g.sS.n
            w2chunk(16 + i)

        _wfin(g, blk, g.sabs[:, 0:NCC], float(C) * NATT, 1e-8, 0, "wa")

        @blk.vector
        def _(v, w=None):
            v.wait_ge(g.sS.h, s_w2a[NJ - 1])
        _wfin(g, blk, g.sabs[:, 48:48 + NJ], float(HID) * C, 1e-5, 6, "w2")
        g.w2fin_mark = g.sV.n

        @blk.vector
        def _(v):
            g.sQ.wait(v)
            ch = chained(g, v)
            ch.tensor_scalar(g.srow[0:1, 20:21], g.srow[0:1, 0:1], 0.5, None,
                             Alu.mult)
            ch.tensor_scalar(g.srow[0:1, 21:22], g.srow[0:1, 0:1], 1.5, None,
                             Alu.mult)
            ch.tensor_scalar(g.srow[0:1, 22:23], g.srow[0:1, 0:1], -0.5, None,
                             Alu.mult)
            g.sV.inc(v.tensor_scalar(g.srow[0:1, 23:24], g.srow[0:1, 0:1],
                                     -1.5, None, Alu.mult))
        v_th = g.sV.n

        with nc.psum_tensor("pth", [128, 512], F32) as pth:
            @blk.tensor
            def _(t):
                g.sV.wait_at(t, v_th)
                g.sT.inc(t.matmul(pth[:, 0:4], g.onesrow[:, :],
                                  g.srow[0:1, 20:24], start=True, stop=True))
            t_th = g.sT.n

            @blk.vector
            def _(v):
                g.sT.wait_at(v, t_th)
                v.tensor_copy(g.bc[:, 13:16], pth[:, 0:3])
                g.sV.inc(v.tensor_copy(g.bc[:, 9:10], pth[:, 3:4]))
        v_thb = g.sV.n

        d_waq = [0] * NCC
        v_wq = [0] * NCC
        d_wqo = [0] * NCC
        for cc in range(NCC):
            @blk.scalar
            def _(s, cc=cc):
                if cc < 2:
                    g.sS.wait_at(s, s_w2a[NJ - 2 + cc])
                else:
                    g.sV.wait_at(s, v_wq[cc - 2])
                d_waq[cc] = g.dwa.inc(
                    s.dma_start(out=wsb[:, cc % 2, :], in_=wa_r[cc]), cc % 2)

            @blk.vector
            def _(v, cc=cc):
                g.dwa.wait(v, d_waq[cc])
                if cc == 0:
                    g.sV.wait_at(v, v_thb)
                if cc >= 2:
                    g.dwqo.wait(v, d_wqo[cc - 2])
                wt = wsb[:, cc % 2, :]
                v.tensor_scalar(wpos[:, :], wt, g.bc[:, 13:14], None, Alu.is_ge)
                v.scalar_tensor_tensor(wpos[:, :], wt, g.bc[:, 14:15],
                                       wpos[:, :], Alu.is_ge, Alu.add)
                v.tensor_scalar(wneg[:, :], wt, g.bc[:, 15:16], None, Alu.is_le)
                v.scalar_tensor_tensor(wneg[:, :], wt, g.bc[:, 9:10],
                                       wneg[:, :], Alu.is_le, Alu.add)
                g.sV.inc(v.tensor_tensor(wqab[:, cc % 2, :], wpos[:, :],
                                         wneg[:, :], Alu.subtract))
                v_wq[cc] = g.sV.n

            @blk.scalar
            def _(s, cc=cc):
                g.sV.wait_at(s, v_wq[cc])
                d_wqo[cc] = g.dwqo.inc(
                    s.dma_start(out=g.wqascr[cc], in_=wqab[:, cc % 2, :]), cc % 2)
                g.d_wqo_last[cc % 2] = d_wqo[cc]

        d_wp = [0] * NCC
        s_wp = [0] * NCC
        for cc in range(NCC):
            @blk.scalar
            def _(s, cc=cc):
                if cc >= 2:
                    g.sS.wait_at(s, s_wp[cc - 2])
                d_wp[cc] = g.dwp.inc(s.dma_start(out=wstr[:, cc % 2, :],
                                                 in_=wp_r[cc]), cc % 2)

            @blk.scalar
            def _(s, cc=cc):
                g.dwp.wait(s, d_wp[cc])
                g.sS.inc(s.activation(sink3[:, 0:C], wstr[:, cc % 2, :], AF.Abs,
                                      accum_out=g.sabs[:, 8 + cc:9 + cc]))
                s_wp[cc] = g.sS.n

    if g.debug:
        _dbg_chunks(g, g.dbg["dbg_xnT"], g.xnT, NCC, "xnT")
        _dbg_chunks(g, g.dbg["dbg_aT"], g.aT, NCC, "aT")
def _p1(g, R):
    nc = g.nc
    d_wqo_last = g.d_wqo_last
    wqa = R("wqa", [128, NCC, 1536], BF16)
    wstr1 = R("wstr1", [128, 4, 512], F32)
    w2p1 = R("w2p1", [128, 512], F32)
    w2n1 = R("w2n1", [128, 512], F32)
    w2qb1 = R("w2qb1", [128, 2, 512], BF16)
    w2_r = g.w2T_d.rearrange("(n p) o -> n p o", p=128)

    d_w2q = [0] * (2 * NJ)
    v_w2q = [0] * (2 * NJ)
    d_w2qo = [0, 0]
    wsteps = []

    def mk_dma(k):
        def dma(blk):
            @blk.scalar
            def _(s):
                if k >= 4:
                    g.sV.wait_at(s, v_w2q[k - 4])
                d_w2q[k] = g.dw23.inc(s.dma_start(
                    out=wstr1[:, k % 4, :],
                    in_=w2_r[k // 2][:, 512 * (k % 2):512 * (k % 2) + 512]),
                    k % 4)
        return dma

    def mk_w2q(k, jj, hf):
        def quant(blk):
            @blk.vector
            def _(v):
                g.dw23.wait(v, d_w2q[k])
                g.dw2qo.wait(v, d_w2qo[k % 2])
                v.tensor_scalar(w2p1[:, :], wstr1[:, k % 4, :], g.bc[:, 7:8],
                                0.5, Alu.mult, Alu.is_ge)
                v.tensor_scalar(w2n1[:, :], wstr1[:, k % 4, :], g.bc[:, 7:8],
                                -0.5, Alu.mult, Alu.is_le)
                g.sV.inc(v.tensor_tensor(w2qb1[:, k % 2, :], w2p1[:, :],
                                         w2n1[:, :], Alu.subtract))
                v_w2q[k] = g.sV.n
        def out(blk):
            @blk.scalar
            def _(s):
                g.sV.wait_at(s, v_w2q[k])
                d_w2qo[k % 2] = g.dw2qo.inc(s.dma_start(
                    out=g.w2qscr[jj][:, 512 * hf:512 * hf + 512],
                    in_=w2qb1[:, k % 2, :]), k % 2)
        return [quant, out]

    for k in range(4):
        wsteps.append(mk_dma(k))
    kk = 0
    for jj in range(NJ):
        for hf in range(2):
            q, o = mk_w2q(kk, jj, hf)
            wsteps.append(q)
            wsteps.append(o)
            if kk + 4 < 2 * NJ:
                wsteps.append(mk_dma(kk + 4))
            kk += 1

    wsi = [0]

    def emit_w(blk, k):
        n = 0
        while n < k and wsi[0] < len(wsteps):
            wsteps[wsi[0]](blk)
            wsi[0] += 1
            n += 1

    with nc.Block("p1") as blk:
        @blk.vector
        def _(v):
            v.memset(g.Vb[:, :, :], 0.0)
            g.sV.inc(v.memset(
                g.Vb[:, :, :].rearrange("p n (h w) -> p n h w", w=66)[:, :, :, 64:65],
                1.0))

        e_qk = {}
        for half in range(2):
            @blk.sync
            def _(sy, half=half):
                if half == 1:
                    g.sE.wait_at(sy, e_qk[11])
                g.dwqo.wait(sy, d_wqo_last[0])
                g.dwqo.wait(sy, d_wqo_last[1])
                g.dwqa.inc(sy.dma_start(
                    out=wqa[:, :, :],
                    in_=g.wqascr[:, :, 1536 * half:1536 * half + 1536]
                        .rearrange("n p o -> p n o")), 0)
            d_w = (0, g.dwqa.sems[0].n)

            with nc.psum_tensor("pqk0_%d" % half, [128, T], F32) as q0, \
                 nc.psum_tensor("pqk1_%d" % half, [128, T], F32) as q1:
                pq = [q0, q1]
                ocs = list(range(12)) if half == 0 else list(range(12, 16))
                for idx, oc in enumerate(ocs):
                    off = 128 * oc - 1536 * half

                    @blk.tensor
                    def _(t, oc=oc, off=off, d_w=d_w, idx=idx):
                        g.dwqa.wait(t, d_w)
                        if idx >= 2:
                            g.sE.wait_at(t, e_qk[oc - 2])
                        for cc in range(NCC):
                            lhsT = wqa[:, cc, off:off + 128]
                            for th in range(4):
                                ins = t.matmul(pq[oc % 2][:, 512 * th:512 * th + 512],
                                         lhsT,
                                         g.xnT[:, cc, 512 * th:512 * th + 512],
                                         start=(cc == 0), stop=(cc == NCC - 1))
                        g.sT.inc(ins)
                    tm = g.sT.n

                    @blk.vector
                    def _(v, oc=oc, tm=tm):
                        g.sT.wait_at(v, tm)
                        dst = (g.QT[:, oc, :] if oc < 8 else g.KT[:, oc - 8, :])
                        g.sE.inc(v.tensor_copy(dst, pq[oc % 2][:, :]))
                        e_qk[oc] = g.sE.n
                    emit_w(blk, 6)

            if half == 1:
                with nc.psum_tensor("pv0", [128, 1024], F32) as v0, \
                     nc.psum_tensor("pv1", [128, 1024], F32) as v1:
                    pv = [v0, v1]
                    e_v = [0] * NT
                    for tc in range(NT):
                        @blk.tensor
                        def _(t, tc=tc):
                            if tc == 0:
                                g.sE.wait_at(t, e_qk[15])
                            if tc >= 2:
                                g.sE.wait_at(t, e_v[tc - 2])
                            for cc in range(NCC):
                                lhsT = g.xnT[:, cc, 128 * tc:128 * tc + 128]
                                for vh in range(2):
                                    rhs = wqa[:, cc, 512 + 512 * vh:1024 + 512 * vh]
                                    ins = t.matmul(pv[tc % 2][:, 512 * vh:512 * vh + 512],
                                             lhsT, rhs,
                                             start=(cc == 0), stop=(cc == NCC - 1))
                            g.sT.inc(ins)
                        tm = g.sT.n

                        @blk.vector
                        def _(v, tc=tc, tm=tm):
                            g.sT.wait_at(v, tm)
                            g.sE.inc(v.tensor_copy(
                                g.Vb[:, tc, :].rearrange("p (h w) -> p h w",
                                                         w=66)[:, :, 0:64],
                                pv[tc % 2][:, :].rearrange("p (h d) -> p h d",
                                                           d=64)))
                            e_v[tc] = g.sE.n
                        emit_w(blk, 6)

        emit_w(blk, len(wsteps))

    if g.debug:
        _dbg_chunks(g, g.dbg["dbg_QT"], g.QT, NCC, "QT")
        _dbg_chunks(g, g.dbg["dbg_KT"], g.KT, NCC, "KT")
        _dbg_chunks(g, g.dbg["dbg_Vb"], g.Vb, NT, "Vb")
        _dbg_dump2(g, g.dbg["dbg_scales"][:, 0:16], g.bc[:, 0:16], "scales")


def _p2(g, R):
    nc = g.nc
    with nc.Block("p2") as blk:
        @blk.vector
        def _(v):
            ch = chained(g, v)
            ch.tensor_tensor(g.srow[0:1, 16:17], g.srow[0:1, 0:1],
                             g.srow[0:1, 0:1], Alu.mult)
            g.sV.inc(v.tensor_scalar(g.srow[0:1, 16:17], g.srow[0:1, 16:17],
                                      1.0 / 8.0, None, Alu.mult))
        _scalar_bcast(g, blk, 16, 8, "alpha")
def _p3(g, R):
    nc = g.nc
    recrow = R("recrow", [1, 512], F32)
    rb = R("rb", [128, 512], F32)
    wstrh = R("wstrh", [128, 2, 512], F32)
    w1_r = g.w1cm_d

    sa = contextlib.ExitStack()
    Ra = lambda *a, **k: sa.enter_context(nc.sbuf_tensor(*a, side="right", **k))
    cmaxw = Ra("cmaxw", [128, 1024], F32)
    cmax256 = Ra("cmax256", [128, 256], F32)
    gma = Ra("gma", [128, 64], F32)
    grow = Ra("grow", [64, 1], F32)
    with nc.Block("p3a") as blk, \
         nc.psum_tensor("pX3", [128, 2048], F32) as pX, \
         nc.psum_tensor("pC3", [128, 512], F32) as pC:
        v_bank = [0, 0, 0, 0]

        @blk.vector
        def _(v):
            g.sV.inc(v.memset(cmaxw[:, :], NEG))

        for h in range(H):
            hc, hh = h // 2, h % 2
            r0, r1 = 64 * hh, 64 * hh + 64
            for qi in range(NT):
                width = 128 * (qi + 1)
                nk = (width + 511) // 512
                t_chunk = [0] * nk

                @blk.tensor
                def _(t, h=h, qi=qi, width=width, nk=nk, hc=hc, r0=r0, r1=r1,
                      vb=tuple(v_bank)):
                    for ck in range(nk):
                        w = min(512, width - 512 * ck)
                        g.sV.wait_at(t, vb[ck])
                        g.sT.inc(t.matmul(
                            pX[:, 512 * ck:512 * ck + w],
                            g.QT[r0:r1, hc, 128 * qi:128 * qi + 128],
                            g.KT[r0:r1, hc, 512 * ck:512 * ck + w],
                            start=True, stop=True))
                        t_chunk[ck] = g.sT.n

                @blk.vector
                def _(v, h=h, qi=qi, width=width, nk=nk, t_chunk=t_chunk,
                      last=(h == H - 1 and qi == NT - 1)):
                    for ck in range(nk):
                        w = min(512, width - 512 * ck)
                        g.sT.wait_at(v, t_chunk[ck])
                        ins = v.reduce_max(
                            cmaxw[:, 64 * h + 4 * qi + ck:64 * h + 4 * qi + ck + 1],
                            pX[:, 512 * ck:512 * ck + w], AX.X)
                        if last and ck == nk - 1:
                            g.sQ.inc(ins)
                        else:
                            g.sV.inc(ins)
                            v_bank[ck] = g.sV.n



        @blk.vector
        def _(v):
            g.sQ.wait(v)
            ch = chained(g, v)
            ch.reduce_max(cmax256[:, :],
                          cmaxw[:, :].rearrange("p (a f) -> p a f", f=4), AX.X)
            g.sV.inc(v.reduce_max(
                gma[:, :], cmax256[:, :].rearrange("p (a f) -> p a f", f=4),
                AX.X))
        vm1 = g.sV.n

        @blk.tensor
        def _(t, vm1=vm1):
            g.sV.wait_at(t, vm1)
            g.sT.inc(t.matmul(pC[0:64, 0:128], gma[:, :], g.identf[:, :],
                              start=True, stop=True))
        tm1 = g.sT.n

        @blk.vector
        def _(v, tm1=tm1):
            g.sT.wait_at(v, tm1)
            ch = chained(g, v)
            ch.reduce_max(grow[:, :], pC[0:64, 0:128], AX.X)
            g.sV.inc(v.tensor_scalar(grow[:, :], grow[:, :], g.bc[0:64, 8:9],
                                     -1.0, Alu.mult, Alu.mult))
        vm2 = g.sV.n

        @blk.tensor
        def _(t, vm2=vm2):
            g.sV.wait_at(t, vm2)
            g.sT.inc(t.matmul(pC[0:1, 128:192], grow[:, :],
                              g.identf[0:64, 0:64], start=True, stop=True))
        tm2 = g.sT.n

        @blk.vector
        def _(v, tm2=tm2):
            g.sT.wait_at(v, tm2)
            g.sV.inc(v.tensor_copy(g.row[0:1, 0:64], pC[0:1, 128:192]))
        vm3 = g.sV.n

        @blk.tensor
        def _(t, vm3=vm3):
            g.sV.wait_at(t, vm3)
            g.sT.inc(t.matmul(pC[:, 256:320], g.onesrow[:, :],
                              g.row[0:1, 0:64], start=True, stop=True))
        tm3 = g.sT.n

        @blk.vector
        def _(v, tm3=tm3):
            g.sT.wait_at(v, tm3)
            g.sV.inc(v.tensor_copy(g.cbrow[:, 0:64], pC[:, 256:320]))

    sa.close()

    with nc.Block("p3b") as blk, \
         nc.psum_tensor("pS0", [128, 512], F32) as pS0, \
         nc.psum_tensor("pS1", [128, 512], F32) as pS1, \
         nc.psum_tensor("pS2", [128, 512], F32) as pS2, \
         nc.psum_tensor("pS3", [128, 512], F32) as pS3, \
         nc.psum_tensor("pS4", [128, 512], F32) as pS4, \
         nc.psum_tensor("pS5", [128, 512], F32) as pS5, \
         nc.psum_tensor("pY", [65, 512], F32) as pY, \
         nc.psum_tensor("pM", [128, 512], F32) as pM:
        pS = [pS0, pS1, pS2, pS3, pS4, pS5]
        ND = 6
        def ptidx(G, j):
            return (G % 2) * 12 + j if j < 12 else 12 + j
        pt_free = [0] * 28
        exp_slot_pred = [0] * ND
        splan_l = [g.sS.n]
        uglob_l = [0]
        e_y_l = [0]
        groups = [(h, gq) for h in range(H) for gq in range(4)]
        ctxs = []

        def emit_ST(G):
            h, gq = groups[G]
            hc, hh = h // 2, h % 2
            r0, r1 = 64 * hh, 64 * hh + 64
            nj = 4 * gq + 4
            units = [(j,) for j in range(nj)]
            ctx = dict(h=h, gq=gq, hc=hc, r0=r0, r1=r1, nj=nj, units=units,
                       t_s=[], s_e_plan=[], slots=[])
            work = []
            for unit in units:
                slot = uglob_l[0] % ND
                ctx["slots"].append(slot)
                uglob_l[0] += 1
                work.append((unit, slot, exp_slot_pred[slot]))
                splan_l[0] += 1
                ctx["s_e_plan"].append(splan_l[0])
                exp_slot_pred[slot] = splan_l[0]

            @blk.tensor
            def _(t, work=work, gq=gq, hc=hc, r0=r0, r1=r1, ctx=ctx):
                for unit, slot, esp in work:
                    if esp > 0:
                        t.wait_ge(g.sS.h, esp)
                    for k, j in enumerate(unit):
                        qo = max(0, 128 * j - 512 * gq)
                        width = 512 - qo
                        diag = j >= 4 * gq
                        ins = t.matmul(
                            pS[slot][:, 512 * k:512 * k + width],
                            g.KT[r0:r1, hc, 128 * j:128 * j + 128],
                            g.QT[r0:r1, hc, 512 * gq + qo:512 * gq + 512],
                            start=True, stop=not diag)
                        if diag:
                            ins = t.matmul(
                                pS[slot][:, 512 * k:512 * k + 128],
                                g.identb[:, :], g.trinb[:, :],
                                start=False, stop=True)
                    g.sT.inc(ins)
                    ctx["t_s"].append(g.sT.n)
            ctxs.append(ctx)

        def emit_EXP(G):
            ctx = ctxs[G]
            ptw = [tuple(pt_free[ptidx(G, j)] for j in unit)
                   for unit in ctx["units"]]

            @blk.scalar
            def _(s, ctx=ctx, ptw=ptw, G=G):
                h, gq = ctx["h"], ctx["gq"]
                for ui, unit in enumerate(ctx["units"]):
                    g.sT.wait_at(s, ctx["t_s"][ui])
                    for pv in ptw[ui]:
                        g.sT.wait_at(s, pv)
                    j0 = unit[0]
                    qo = max(0, 128 * j0 - 512 * gq)
                    g.sS.inc(s.activation(
                        g.PT[:, ptidx(G, j0), qo:512],
                        pS[ctx["slots"][ui]][:, 0:512 - qo], AF.Exp,
                        bias=g.cbrow[:, 4 * h + gq:4 * h + gq + 1],
                        scale=g.bc[:, 8:9]))
                    assert g.sS.n == ctx["s_e_plan"][ui]

        def emit_PV(G):
            ctx = ctxs[G]

            @blk.tensor
            def _(t, ctx=ctx, ey=e_y_l[0], G=G):
                h, gq, nj = ctx["h"], ctx["gq"], ctx["nj"]
                for ui, unit in enumerate(ctx["units"]):
                    t.wait_ge(g.sS.h, ctx["s_e_plan"][ui])
                    if ui == 0:
                        g.sE.wait_at(t, ey)
                    for j in unit:
                        qo = max(0, 128 * j - 512 * gq)
                        ins = t.matmul(pY[:, qo:512],
                                       g.Vb[:, j, 66 * h:66 * h + 65],
                                       g.PT[:, ptidx(G, j), qo:512],
                                       start=(j == 0), stop=(j == nj - 1))
                    g.sT.inc(ins)
                    for j in unit:
                        pt_free[ptidx(G, j)] = g.sT.n
            ctx["t_pv"] = g.sT.n

        def emit_TAIL(G):
            ctx = ctxs[G]

            @blk.vector
            def _(v, t_pv=ctx["t_pv"]):
                g.sT.wait_at(v, t_pv)
                g.sV.inc(v.reciprocal(recrow[0:1, :], pY[64:65, 0:512]))
            v_rec = g.sV.n

            @blk.tensor
            def _(t, v_rec=v_rec, ey=e_y_l[0]):
                g.sV.wait_at(t, v_rec)
                g.sE.wait_at(t, ey)
                g.sT.inc(t.matmul(pM[0:64, :], g.onesrow[0:1, 0:64],
                                  recrow[0:1, :], start=True, stop=True))
            t_r = g.sT.n

            @blk.vector
            def _(v, t_r=t_r, ctx=ctx):
                g.sT.wait_at(v, t_r)
                v.tensor_copy(rb[0:64, :], pM[0:64, :])
                g.sE.inc(v.tensor_tensor(
                    g.YT[ctx["r0"]:ctx["r1"], ctx["hc"],
                         512 * ctx["gq"]:512 * ctx["gq"] + 512],
                    pY[0:64, :], rb[0:64, :], Alu.mult))
            e_y_l[0] = g.sE.n

        d_w1h = [0] * (2 * NJ)
        s_w1h = [0] * (2 * NJ)

        def mk_w1h(k):
            jj, hf = k // 2, k % 2

            def entry():
                @blk.scalar
                def _(s):
                    if k >= 2:
                        g.sV.wait_at(s, s_w1h[k - 2])
                    d_w1h[k] = g.dw13.inc(s.dma_start(
                        out=wstrh[:, k % 2, :].rearrange("p (a j) -> p a j",
                                                         a=4),
                        in_=w1_r[jj].rearrange("(a p) j -> p a j", p=128)
                            [:, 4 * hf:4 * hf + 4, :]), k % 2)

                @blk.vector
                def _(v):
                    g.dw13.wait(v, d_w1h[k])
                    g.sV.inc(v.reduce_sum(g.sabs[:, 80 + k:81 + k],
                                          wstrh[:, k % 2, :], AX.X,
                                          apply_absolute_value=True))
                    s_w1h[k] = g.sV.n
            return entry

        wentries = [mk_w1h(k) for k in range(2 * NJ)]
        wei = [0]

        def emit_we(n):
            c = 0
            while c < n and wei[0] < len(wentries):
                wentries[wei[0]]()
                wei[0] += 1
                c += 1

        emit_ST(0)
        emit_EXP(0)
        for G in range(1, len(groups)):
            emit_ST(G)
            emit_EXP(G)
            emit_PV(G - 1)
            emit_TAIL(G - 1)
            emit_we(2)
        emit_PV(len(groups) - 1)
        emit_TAIL(len(groups) - 1)
        emit_we(len(wentries))

    if g.debug:
        _dbg_chunks(g, g.dbg["dbg_YT"], g.YT, NCC, "yt")
        _dbg_dump2(g, g.dbg["dbg_cb"], g.cbrow[:, :], "cbrow")
def _p4(g, R):
    nc = g.nc
    wpq = R("wpq", [128, NCC, C], BF16)
    wpf = R("wpf", [128, 2, C], F32)
    wpy = R("wpy", [128, C], F32)
    wpp = R("wpp", [128, C], F32)
    wpn = R("wpn", [128, C], F32)
    wp_r = g.wprojT_d.rearrange("(n p) o -> n p o", p=128)

    with nc.Block("p4") as blk:
        _wfin(g, blk, g.sabs[:, 8:16], float(C) * C, 1e-8, 2, "wp")

        @blk.vector
        def _(v):
            g.sQ.wait(v)
            g.sV.inc(v.tensor_tensor(g.srow[0:1, 17:18], g.srow[0:1, 0:1],
                                     g.srow[0:1, 2:3], Alu.mult))
        _scalar_bcast(g, blk, 17, 10, "sap")

        d_wp = [0] * NCC
        v_wp = [0] * NCC
        for cc in range(NCC):
            @blk.scalar
            def _(s, cc=cc):
                if cc >= 2:
                    g.sV.wait_at(s, v_wp[cc - 2])
                d_wp[cc] = g.dwpf.inc(s.dma_start(out=wpf[:, cc % 2, :],
                                                  in_=wp_r[cc]), cc % 2)

            @blk.vector
            def _(v, cc=cc):
                g.dwpf.wait(v, d_wp[cc])
                v.tensor_scalar(wpy[:, :], wpf[:, cc % 2, :], g.bc[:, 3:4],
                                None, Alu.mult)
                v.tensor_scalar(wpp[:, :], wpy[:, :], 0.5, None, Alu.is_ge)
                v.scalar_tensor_tensor(wpp[:, :], wpy[:, :], 1.5, wpp[:, :],
                                       Alu.is_ge, Alu.add)
                v.tensor_scalar(wpn[:, :], wpy[:, :], -0.5, None, Alu.is_le)
                v.scalar_tensor_tensor(wpn[:, :], wpy[:, :], -1.5, wpn[:, :],
                                       Alu.is_le, Alu.add)
                g.sV.inc(v.tensor_tensor(wpq[:, cc, :], wpp[:, :], wpn[:, :],
                                         Alu.subtract))
                v_wp[cc] = g.sV.n

        with nc.psum_tensor("pP0", [128, T], F32) as pp0, \
             nc.psum_tensor("pP1", [128, T], F32) as pp1:
            pp = [pp0, pp1]
            e_p = [0] * NCC
            for oc in range(NCC):
                @blk.tensor
                def _(t, oc=oc):
                    g.sV.wait_at(t, v_wp[NCC - 1])
                    if oc >= 2:
                        g.sE.wait_at(t, e_p[oc - 2])
                    for cc in range(NCC):
                        lhsT = wpq[:, cc, 128 * oc:128 * oc + 128]
                        for th in range(4):
                            ins = t.matmul(pp[oc % 2][:, 512 * th:512 * th + 512],
                                     lhsT, g.YT[:, cc, 512 * th:512 * th + 512],
                                     start=(cc == 0), stop=(cc == NCC - 1))
                    g.sT.inc(ins)
                tm = g.sT.n

                @blk.vector
                def _(v, oc=oc, tm=tm):
                    g.sT.wait_at(v, tm)
                    g.sE.inc(v.tensor_scalar(g.attnT[:, oc, :], pp[oc % 2][:, :],
                                    g.bc[:, 10:11], None, Alu.mult))
                    e_p[oc] = g.sE.n

    if g.debug:
        _dbg_chunks(g, g.dbg["dbg_attnT"], g.attnT, NCC, "attnT")


def _p5(g, R):
    nc = g.nc
    g.d_h3o_last = [0, 0]
    A = R("Abuf", [128, 2, T + 2], F32)
    B = R("Bbuf", [128, 2, T], F32)
    Cb2 = R("Cbuf", [128, 2, T], F32)
    Mrun = R("MrunB", [128, T], F32)
    w1f = R("w1f", [128, 2, C], F32)
    w1p = R("w1p", [128, C], F32)
    w1n = R("w1n", [128, C], F32)
    w1q = R("w1q", [128, 2, C], BF16)
    w1_r = g.w1cm_d

    with nc.Block("p5pre") as blk:
        _wfin(g, blk, g.sabs[:, 80:80 + 2 * NJ], float(HID) * C, 1e-5, 4, "w1")

        @blk.vector
        def _(v):
            g.sQ.wait(v)
            ch = chained(g, v)
            ch.tensor_scalar(g.srow[0:1, 18:19], g.srow[0:1, 4:5],
                             1.0 / 127.0, None, Alu.mult)
            ch.tensor_scalar(g.srow[0:1, 24:25], g.srow[0:1, 4:5], 0.5, None,
                             Alu.mult)
            g.sV.inc(v.tensor_scalar(g.srow[0:1, 25:26], g.srow[0:1, 4:5],
                                     -0.5, None, Alu.mult))
        _scalar_bcast(g, blk, 18, 11, "w1f127")

        vth = g.sV.n
        with nc.psum_tensor("pth5", [128, 512], F32) as pth:
            @blk.tensor
            def _(t, vth=vth):
                g.sV.wait_at(t, vth)
                g.sT.inc(t.matmul(pth[:, 0:2], g.onesrow[:, :],
                                  g.srow[0:1, 24:26], start=True, stop=True))
            tth = g.sT.n

            @blk.vector
            def _(v, tth=tth):
                g.sT.wait_at(v, tth)
                g.sV.inc(v.tensor_copy(g.bc[:, 16:18], pth[:, 0:2]))

        @blk.vector
        def _(v):
            VC(g, v, v.tensor_scalar(g.sm[:, 8:8 + NT], g.clipcol[:, :],
                                     g.bc[:, 11:12], None, Alu.mult))
            v.memset(Mrun[:, :], 0.0)
            v.memset(A[:, :, 0:1], 0.0)
            g.sV.inc(v.memset(A[:, :, T + 1:T + 2], 0.0))
        vm0 = g.sV.n

        with nc.psum_tensor("ptc", [1, 512], F32) as ptc, \
             nc.psum_tensor("pic", [128, 512], F32) as pic:
            last_v = vm0
            for i in range(NT):
                @blk.tensor
                def _(t, i=i, last_v=last_v):
                    g.sV.wait_at(t, last_v)
                    g.sT.inc(t.matmul(ptc[0:1, 0:128], g.sm[:, 8 + i:9 + i],
                                      g.identf[:, :], start=True, stop=True))
                tm = g.sT.n

                @blk.vector
                def _(v, tm=tm):
                    g.sT.wait_at(v, tm)
                    g.sV.inc(v.tensor_copy(g.row[0:1, 0:128], ptc[0:1, 0:128]))
                vm2 = g.sV.n

                @blk.tensor
                def _(t, vm2=vm2):
                    g.sV.wait_at(t, vm2)
                    g.sT.inc(t.matmul(pic[:, 0:128], g.onesrow[:, :], g.row[0:1, 0:128],
                             start=True, stop=True))
                tm2 = g.sT.n

                @blk.vector
                def _(v, i=i, tm2=tm2):
                    g.sT.wait_at(v, tm2)
                    g.sV.inc(v.tensor_copy(g.inv1b[:, 128 * i:128 * i + 128], pic[:, 0:128]))
                last_v = g.sV.n

    with nc.Block("p5") as blk, \
         nc.psum_tensor("pH0", [128, T], F32) as ph0, \
         nc.psum_tensor("pH1", [128, T], F32) as ph1:
        ph = [ph0, ph1]
        d_w1 = [0] * NJ
        g_w1 = [0] * NJ
        v_w1 = [0] * NJ
        t_h1 = [0] * NJ
        e_h1 = [0] * NJ
        v_conv = [0] * NJ
        v_mr = [0] * NJ
        s_silu = [0] * NJ
        d_h3 = [0] * NJ

        @blk.scalar
        def _(s):
            for j in range(2):
                d_w1[j] = g.dw15.inc(s.dma_start(
                    out=w1f[:, j, :].rearrange("p (a jj) -> p a jj", a=NCC),
                    in_=w1_r[j].rearrange("(a p) jj -> p a jj", p=128)), j)

        def emit_quant(j):
            @blk.vector
            def _(v, j=j):
                g.dw15.wait(v, d_w1[j])
                if j >= 2:
                    g.sT.wait_at(v, t_h1[j - 2])
                v.tensor_scalar(w1p[:, :], w1f[:, j % 2, :], g.bc[:, 16:17],
                                None, Alu.is_ge)
                v.tensor_scalar(w1n[:, :], w1f[:, j % 2, :], g.bc[:, 17:18],
                                None, Alu.is_le)
                g.sV.inc(v.tensor_tensor(w1q[:, j % 2, :], w1p[:, :],
                                         w1n[:, :], Alu.subtract))
                v_w1[j] = g.sV.n

            @blk.scalar
            def _(s, j=j):
                if j + 2 < NJ:
                    g.sV.wait_at(s, v_w1[j])
                    d_w1[j + 2] = g.dw15.inc(s.dma_start(
                        out=w1f[:, j % 2, :].rearrange("p (a jj) -> p a jj", a=NCC),
                        in_=w1_r[j + 2].rearrange("(a p) jj -> p a jj", p=128)),
                        j % 2)

        emit_quant(0)
        emit_quant(1)
        for j in range(NJ):
            @blk.tensor
            def _(t, j=j):
                g.sV.wait_at(t, v_w1[j])
                if j >= 2:
                    g.sE.wait_at(t, e_h1[j - 2])
                for cc in range(NCC):
                    lhsT = w1q[:, j % 2, 128 * cc:128 * cc + 128]
                    for th in range(4):
                        ins = t.matmul(ph[j % 2][:, 512 * th:512 * th + 512],
                                 lhsT, g.aT[:, cc, 512 * th:512 * th + 512],
                                 start=(cc == 0), stop=(cc == NCC - 1))
                g.sT.inc(ins)
                t_h1[j] = g.sT.n

            @blk.vector
            def _(v, j=j):
                g.sT.wait_at(v, t_h1[j])
                g.sE.inc(v.tensor_tensor(A[:, j % 2, 1:T + 1], ph[j % 2][:, :],
                                         g.inv1b[:, :], Alu.mult))
                e_h1[j] = g.sE.n
                if j >= 2:
                    g.sS.wait_at(v, s_silu[j - 2])
                    g.sV.wait_at(v, v_mr[j - 2])
                v.tensor_scalar(B[:, j % 2, :], A[:, j % 2, 1:T + 1],
                                g.cwb[:, j, 1:2], g.cbb[:, j:j + 1],
                                Alu.mult, Alu.add)
                v.scalar_tensor_tensor(B[:, j % 2, :], A[:, j % 2, 0:T],
                                       g.cwb[:, j, 0:1], B[:, j % 2, :],
                                       Alu.mult, Alu.add)
                g.sV.inc(v.scalar_tensor_tensor(B[:, j % 2, :], A[:, j % 2, 2:T + 2],
                                                g.cwb[:, j, 2:3], B[:, j % 2, :],
                                                Alu.mult, Alu.add))
                v_conv[j] = g.sV.n

            @blk.scalar
            def _(s, j=j):
                g.sV.wait_at(s, v_conv[j])
                if j >= 2:
                    g.dh3o.wait(s, d_h3[j - 2])
                s.activation(Cb2[:, j % 2, :], B[:, j % 2, :], AF.Silu)
                g.sS.inc(s.activation(B[:, j % 2, :], Cb2[:, j % 2, :], AF.Abs))
                s_silu[j] = g.sS.n

            if j + 2 < NJ:
                emit_quant(j + 2)

            @blk.vector
            def _(v, j=j):
                g.sS.wait_at(v, s_silu[j])
                g.sV.inc(v.tensor_tensor(Mrun[:, :], Mrun[:, :], B[:, j % 2, :],
                                         Alu.max))
                v_mr[j] = g.sV.n

            @blk.scalar
            def _(s, j=j):
                g.sS.wait_at(s, s_silu[j])
                d_h3[j] = g.dh3o.inc(s.dma_start(out=g.h3scr[j],
                                                 in_=Cb2[:, j % 2, :]), j % 2)
                g.d_h3o_last[j % 2] = d_h3[j]

    with nc.Block("p6pre") as blk:
        @blk.vector
        def _(v):
            g.sV.inc(v.tensor_scalar(g.srow[0:1, 19:20], g.srow[0:1, 6:7], 1.0 / 127.0,
                            None, Alu.mult))
        _scalar_bcast(g, blk, 19, 12, "w2f127")

    with nc.Block("p6") as blk, \
         nc.psum_tensor("pm6", [128, 2048], F32) as pm6, \
         nc.psum_tensor("prR", [1, 2048], F32) as prR:
        vb12 = 0
        @blk.tensor
        def _(t):
            for i in range(NT):
                ins = t.matmul(pm6[:, 128 * i:128 * i + 128],
                               Mrun[:, 128 * i:128 * i + 128], g.identf[:, :],
                               start=(i % 4 == 0), stop=(i % 4 == 3))
            g.sT.inc(ins)
        t6a = g.sT.n

        @blk.vector
        def _(v, t6a=t6a, vb12=vb12):
            g.sT.wait_at(v, t6a)
            g.sV.wait_at(v, vb12)
            ch = chained(g, v)
            ch.reduce_max(g.sm[:, 0:16],
                          pm6[:, :].rearrange("p (a c) -> p a c", c=128), AX.X)
            ch.tensor_scalar(g.sm[:, 0:16], g.sm[:, 0:16], 1e-5, None, Alu.max)
            ch.reciprocal(g.sm[:, 16:32], g.sm[:, 0:16])
            ch.tensor_scalar(g.sm[:, 16:32], g.sm[:, 16:32], 127.0, None,
                             Alu.mult)
            g.sV.inc(v.tensor_scalar(g.sm[:, 0:16], g.sm[:, 0:16],
                                     g.bc[:, 12:13], None, Alu.mult))
        v6a = g.sV.n

        rowb = B[0:1, 0, :]
        vprev = v6a
        for rnd, (colr, dst) in enumerate([(g.sm[:, 16:32], g.s2b),
                                           (g.sm[:, 0:16], g.inv2b)]):
            @blk.tensor
            def _(t, vprev=vprev, colr=colr):
                g.sV.wait_at(t, vprev)
                for r in range(NT):
                    ins = t.matmul(prR[0:1, 128 * r:128 * r + 128],
                                   colr[:, r:r + 1], g.identf[:, :],
                                   start=(r % 4 == 0), stop=(r % 4 == 3))
                g.sT.inc(ins)
            t6b = g.sT.n

            @blk.vector
            def _(v, t6b=t6b):
                g.sT.wait_at(v, t6b)
                g.sV.inc(v.tensor_copy(rowb, prR[0:1, :]))
            v6b = g.sV.n

            @blk.tensor
            def _(t, v6b=v6b):
                g.sV.wait_at(t, v6b)
                for r in range(NT):
                    ins = t.matmul(pm6[:, 128 * r:128 * r + 128],
                                   g.onesrow[0:1, :],
                                   rowb[0:1, 128 * r:128 * r + 128],
                                   start=(r % 4 == 0), stop=(r % 4 == 3))
                g.sT.inc(ins)
            t6c = g.sT.n

            @blk.vector
            def _(v, t6c=t6c, dst=dst):
                g.sT.wait_at(v, t6c)
                ins = None
                for b in range(4):
                    ins = v.tensor_copy(dst[:, 512 * b:512 * b + 512],
                                        pm6[:, 512 * b:512 * b + 512])
                g.sV.inc(ins)
            vprev = g.sV.n

    if g.debug:
        _dbg_dump2(g, g.dbg["dbg_s2b"], g.s2b[:, :], "s2b")
def _p7(g, R):
    nc = g.nc
    W2q = R("W2q", [128, NJ, C], BF16)
    h3b = R("h3b", [128, 4, 512], F32)
    bvF = R("bvF", [128, 8, 512], F32)
    bqi = R("bqi", [128, 512], I32)
    bcf = R("bcf", [128, 4, 512], F32)
    bp_ = R("bp_", [128, 512], F32)
    bn_ = R("bn_", [128, 512], F32)
    bq = R("bq", [128, 2, 512], BF16)
    NB = 4

    with nc.Block("p7") as blk, \
         nc.psum_tensor("pO", [128, 8 * 512], F32) as pO:
        @blk.sync
        def _(sy):
            g.dw2qo.final_wait(sy)
            g.dw2q7.inc(sy.dma_start(out=W2q[:, :, :],
                                     in_=g.w2qscr.rearrange("n p o -> p n o")), 0)
        d_w2q = (0, g.dw2q7.sems[0].n)

        e_out = 0
        prev_v_bq = [0] * NJ
        prev_t_mm = [0] * NJ
        prev_s_cf = [0] * NJ
        for tt_ in range(4):
            d_h3 = [0] * NJ
            v_bv = [0] * NJ
            s_cf = [0] * NJ
            v_bq = [0] * NJ
            t_mm = [0] * NJ

            @blk.sync
            def _(sy, tt_=tt_):
                if tt_ == 0:
                    g.dh3o.final_wait(sy)
                for j in range(NB):
                    if tt_ > 0:
                        g.sV.wait_at(sy, prev_v_bq[NJ - NB + j])
                    d_h3[j] = g.dh3i.inc(sy.dma_start(
                        out=h3b[:, j, :],
                        in_=g.h3scr[j][:, 512 * tt_:512 * tt_ + 512]), j % 4)

            def emit_bv(b):
                @blk.vector
                def _(v, b=b, tt_=tt_):
                    for j in range(NB * b, NB * b + NB):
                        g.dh3i.wait(v, d_h3[j])
                        if j >= 8:
                            g.sV.wait_at(v, v_bq[j - 8])
                        elif tt_ > 0:
                            g.sV.wait_at(v, prev_v_bq[NJ - 8 + j])
                        g.sV.inc(v.tensor_tensor(
                            bvF[:, j % 8, :], h3b[:, j % NB, :],
                            g.s2b[:, 512 * tt_:512 * tt_ + 512], Alu.mult))
                        v_bv[j] = g.sV.n

                @blk.sync
                def _(sy, b=b, tt_=tt_):
                    for j in range(NB * b, NB * b + NB):
                        if j + NB < NJ:
                            g.sV.wait_at(sy, v_bv[j])
                            d_h3[j + NB] = g.dh3i.inc(sy.dma_start(
                                out=h3b[:, j % NB, :],
                                in_=g.h3scr[j + NB][:, 512 * tt_:512 * tt_ + 512]),
                                j % 4)

            emit_bv(0)
            for b in range(NJ // NB):
                @blk.scalar
                def _(s, b=b, tt_=tt_):
                    for j in range(NB * b, NB * b + NB):
                        g.sV.wait_at(s, v_bv[j])
                        if b > 0:
                            g.sV.wait_at(s, v_bq[j - NB])
                        elif tt_ > 0:
                            g.sV.wait_at(s, prev_v_bq[NJ - NB + j])
                        s.activation(bqi[:, :], bvF[:, j % 8, :],
                                     AF.Copy)
                        g.sS.inc(s.activation(bcf[:, j % NB, :],
                                              bqi[:, :], AF.Copy))
                        s_cf[j] = g.sS.n

                if b + 1 < NJ // NB:
                    emit_bv(b + 1)

                for j in range(NB * b, NB * b + NB):
                    @blk.vector
                    def _(v, j=j, tt_=tt_):
                        g.sS.wait_at(v, s_cf[j])
                        if j >= 2:
                            g.sT.wait_at(v, t_mm[j - 2])
                        elif tt_ > 0:
                            g.sT.wait_at(v, prev_t_mm[NJ - 2 + j])
                        v.scalar_tensor_tensor(bp_[:, :], bvF[:, j % 8, :],
                                               -0.5, bcf[:, j % NB, :],
                                               Alu.add, Alu.is_ge)
                        v.scalar_tensor_tensor(bn_[:, :], bvF[:, j % 8, :],
                                               0.5, bcf[:, j % NB, :],
                                               Alu.add, Alu.is_le)
                        v.tensor_tensor(bcf[:, j % NB, :], bcf[:, j % NB, :],
                                        bp_[:, :], Alu.add)
                        g.sV.inc(v.tensor_tensor(bq[:, j % 2, :],
                                                 bcf[:, j % NB, :],
                                                 bn_[:, :], Alu.subtract))
                        v_bq[j] = g.sV.n

                    @blk.tensor
                    def _(t, j=j):
                        g.sV.wait_at(t, v_bq[j])
                        if j == 0:
                            g.dw2q7.wait(t, d_w2q)
                            g.sE.wait_at(t, e_out)
                        for oc in range(NCC):
                            ins = t.matmul(pO[:, 512 * oc:512 * oc + 512],
                                     W2q[:, j, 128 * oc:128 * oc + 128],
                                     bq[:, j % 2, :],
                                     start=(j == 0), stop=(j == NJ - 1))
                        g.sT.inc(ins)
                        t_mm[j] = g.sT.n

            @blk.vector
            def _(v, tt_=tt_, t_mm=t_mm):
                g.sT.wait_at(v, t_mm[NJ - 1])
                for oc in range(NCC):
                    v.tensor_tensor(bp_[:, :], pO[:, 512 * oc:512 * oc + 512],
                                    g.inv2b[:, 512 * tt_:512 * tt_ + 512], Alu.mult)
                    ins = v.tensor_tensor(g.attnT[:, oc, 512 * tt_:512 * tt_ + 512],
                                    bp_[:, :],
                                    g.attnT[:, oc, 512 * tt_:512 * tt_ + 512],
                                    Alu.add)
                g.sE.inc(ins)
            e_out = g.sE.n
            prev_v_bq = v_bq
            prev_t_mm = t_mm
            prev_s_cf = s_cf


def _p8(g, R):
    nc = g.nc
    xr = R("xr8", [128, 4, C], F32)
    ob = R("ob8", [128, 8, C], F32)
    x_r = g.x_d.rearrange("(n p) c -> n p c", p=128)
    out_r = g.out_d.rearrange("(n p) c -> n p c", p=128)

    with nc.Block("p8") as blk, \
         nc.psum_tensor("pX0", [128, 1024], F32) as px0, \
         nc.psum_tensor("pX1", [128, 1024], F32) as px1:
        px = [px0, px1]
        d_x = [0] * 4
        v_add = [0] * NT
        d_out = [0] * 4

        @blk.sync
        def _(sy):
            d_x[0] = g.dx8.inc(sy.dma_start(
                out=xr[:, :, :], in_=x_r[0:4].rearrange("n p c -> p n c")), 0)

        for b4 in range(4):
            for ti in range(4):
                i = 4 * b4 + ti

                @blk.tensor
                def _(t, i=i):
                    if i >= 2:
                        g.sV.wait_at(t, v_add[i - 2])
                    for oc in range(NCC):
                        ins = t.matmul(px[i % 2][:, 128 * oc:128 * oc + 128],
                                       g.attnT[:, oc, 128 * i:128 * i + 128],
                                       g.identf[:, :],
                                       start=(oc % 4 == 0), stop=(oc % 4 == 3))
                    g.sT.inc(ins)
                tm = g.sT.n

                @blk.vector
                def _(v, i=i, ti=ti, b4=b4, tm=tm):
                    g.sT.wait_at(v, tm)
                    g.dx8.wait(v, d_x[b4])
                    if b4 >= 2:
                        g.dout.wait(v, d_out[b4 - 2])
                    g.sV.inc(v.tensor_tensor(ob[:, 4 * (b4 % 2) + ti, :], px[i % 2][:, :],
                                    xr[:, ti, :], Alu.add))
                    v_add[i] = g.sV.n

            @blk.sync
            def _(sy, b4=b4):
                if b4 + 1 < 4:
                    g.sV.wait_at(sy, v_add[4 * b4 + 3])
                    d_x[b4 + 1] = g.dx8.inc(sy.dma_start(
                        out=xr[:, :, :],
                        in_=x_r[4 * b4 + 4:4 * b4 + 8].rearrange("n p c -> p n c")),
                        0)

            @blk.scalar
            def _(s, b4=b4):
                g.sV.wait_at(s, v_add[4 * b4 + 3])
                d_out[b4] = g.dout.inc(s.dma_start(
                    out=out_r[4 * b4:4 * b4 + 4].rearrange("n p c -> p n c"),
                    in_=ob[:, 4 * (b4 % 2):4 * (b4 % 2) + 4, :]), b4 % 2)

        @blk.sync
        def _(sy):
            for ds in g.dstreams.values():
                ds.final_wait(sy)
            g.sD3.wait(sy)


def kernel(L0_attn_x, L0_attn_w_attn, L0_attn_w_proj, L0_ffn_w1, L0_ffn_conv_w,
           L0_ffn_conv_b, L0_ffn_w2):
    x = np.ascontiguousarray(np.asarray(L0_attn_x, dtype=np.float32))
    wattnT = np.ascontiguousarray(np.asarray(L0_attn_w_attn, np.float32).T)
    wprojT = np.ascontiguousarray(np.asarray(L0_attn_w_proj, np.float32).T)
    w1 = np.asarray(L0_ffn_w1, np.float32)
    w1cm = np.ascontiguousarray(w1.T.reshape(C, NJ, 128).transpose(1, 0, 2))
    w2T = np.ascontiguousarray(np.asarray(L0_ffn_w2, np.float32).T)
    convw = np.ascontiguousarray(
        np.asarray(L0_ffn_conv_w, np.float32).reshape(NJ, 128, 3))
    convb = np.ascontiguousarray(
        np.asarray(L0_ffn_conv_b, np.float32).reshape(NJ, 128))

    nc = build_nc()
    in_maps = [{"x": np.ascontiguousarray(x[b]), "wattnT": wattnT,
                "wprojT": wprojT, "w1cm": w1cm, "w2T": w2T,
                "convw": convw, "convb": convb} for b in range(8)]
    res = run_bass_kernel_spmd(nc, in_maps, core_ids=list(range(8)))
    return np.stack([res.results[b]["out"] for b in range(8)], axis=0)
